# version 05
# baseline (speedup 1.0000x reference)
import numpy as np
import ml_dtypes
from contextlib import ExitStack
import concourse.bass as bass
import concourse.mybir as mybir
from concourse.bass_utils import run_bass_kernel_spmd

F32 = mybir.dt.float32
BF16 = mybir.dt.bfloat16
AF = mybir.ActivationFunctionType
ALU = mybir.AluOpType
AXX = mybir.AxisListType.X

L = 4096
D = 1024
DIN = 2048
NH = 32
HP = 64
NST = 128
DIP = 4384
NZT = 16
NXT = 18
XPW = 520
SCL = 512
NCH = 4
Q = 128
E = 8
HF = 4096


class Ch:
    def __init__(self, ctx, nc, name, step=1):
        self.sem = ctx.enter_context(nc.semaphore(name))
        self.v = 0
        self.step = step

    def inc(self, ins):
        self.v += self.step
        ins.then_inc(self.sem, self.step)
        return (self, self.v)

    def tok(self):
        return (self, self.v)


def build_program(n_sc=8, do_ffn=True, debug=(), n_pairs=4):
    nc = bass.Bass("TRN2", target_bir_lowering=False)

    x_h = nc.dram_tensor("x", [L, D], F32, kind="ExternalInput")
    wipT_h = nc.dram_tensor("wipT", [D, DIP], F32, kind="ExternalInput")
    cdiag_h = nc.dram_tensor("cdiag", [NXT * 4 * 128, 128], F32, kind="ExternalInput")
    cbT_h = nc.dram_tensor("cbT", [128, NXT], F32, kind="ExternalInput")
    dtb_h = nc.dram_tensor("dtb", [NH, 1], F32, kind="ExternalInput")
    alog_h = nc.dram_tensor("alog", [NH, 1], F32, kind="ExternalInput")
    dbc_h = nc.dram_tensor("dbc", [128, NH], F32, kind="ExternalInput")
    wopT_h = nc.dram_tensor("wopT", [DIN, D], F32, kind="ExternalInput")
    nwT_h = nc.dram_tensor("nwT", [128, 16], F32, kind="ExternalInput")
    w1_h = nc.dram_tensor("w1", [E * D, HF], F32, kind="ExternalInput")
    b1T_h = nc.dram_tensor("b1T", [E * 128, 32], F32, kind="ExternalInput")
    w2_h = nc.dram_tensor("w2", [E * HF, D], F32, kind="ExternalInput")
    b2_h = nc.dram_tensor("b2", [E, D], F32, kind="ExternalInput")
    wr_h = nc.dram_tensor("wr", [D, E], F32, kind="ExternalInput")
    br_h = nc.dram_tensor("br", [1, E], F32, kind="ExternalInput")
    u01_h = nc.dram_tensor("u01", [128, 128], F32, kind="ExternalInput")
    a1s_h = nc.dram_tensor("a1s", [128, 128], F32, kind="ExternalInput")
    idf_h = nc.dram_tensor("idf", [128, 128], F32, kind="ExternalInput")
    ones_h = nc.dram_tensor("ones1", [1, 128], F32, kind="ExternalInput")
    xnt_hbm = nc.dram_tensor("xnt_scr", [8 * 128, L], BF16)
    out_h = nc.dram_tensor("out", [L, D], F32, kind="ExternalOutput")

    dbg = {}
    if "zx" in debug:
        dbg["z"] = nc.dram_tensor("dbg_z", [8 * 128, NZT * SCL], BF16, kind="ExternalOutput")
        dbg["xbc"] = nc.dram_tensor("dbg_xbc", [8 * 128, NXT * XPW], BF16, kind="ExternalOutput")
        dbg["dt"] = nc.dram_tensor("dbg_dt", [8 * NH, SCL], F32, kind="ExternalOutput")
        dbg["sp"] = nc.dram_tensor("dbg_sp", [8 * NH, SCL], F32, kind="ExternalOutput")
        dbg["xn"] = nc.dram_tensor("dbg_xn", [8 * 128, NCH * D], BF16, kind="ExternalOutput")
    if "ssd" in debug:
        dbg["cumt"] = nc.dram_tensor("dbg_cumt", [L, NH], F32, kind="ExternalOutput")
        dbg["ytm"] = nc.dram_tensor("dbg_ytm", [L, DIN], F32, kind="ExternalOutput")
        dbg["yrms"] = nc.dram_tensor("dbg_yrms", [L, DIN], BF16, kind="ExternalOutput")
    if "ffn" in debug:
        dbg["lg"] = nc.dram_tensor("dbg_lg", [128, 256], F32, kind="ExternalOutput")
        dbg["w"] = nc.dram_tensor("dbg_w", [128, 256], F32, kind="ExternalOutput")

    ctx = ExitStack()
    with ctx:
        def sb(name, shape, dt=F32):
            return ctx.enter_context(nc.sbuf_tensor(name, shape, dt))

        def psum(name, shape, dt=F32):
            return ctx.enter_context(nc.psum_tensor(name, shape, dt))

        dX = Ch(ctx, nc, "dX", 16)
        dW = Ch(ctx, nc, "dW", 16)
        dP = Ch(ctx, nc, "dP", 16)
        cT = Ch(ctx, nc, "cT")
        cV = Ch(ctx, nc, "cV")
        cS = Ch(ctx, nc, "cS")
        cG = Ch(ctx, nc, "cG")

        waited = {}

        def W(eng, key, *toks):
            hw = waited.setdefault(key, {})
            for t in toks:
                if t is None:
                    continue
                ch, v = t
                if v > 0 and v > hw.get(ch, 0):
                    eng.wait_ge(ch.sem, v)
                    hw[ch] = v

        def WF(eng, key, ch, v):
            hw = waited.setdefault(key, {})
            if v > 0 and v > hw.get(ch, 0):
                eng.wait_ge(ch.sem, v)
                hw[ch] = v

        c_u01 = sb("c_u01", [128, 128])
        c_a1 = sb("c_a1", [128, 128])
        c_a1b = sb("c_a1b", [128, 128], BF16)
        c_idf = sb("c_idf", [128, 128])
        c_idb = sb("c_idb", [128, 128], BF16)
        c_ones = sb("c_ones", [1, 128])
        c_onesb = sb("c_onesb", [1, 128], BF16)
        c_cbT = sb("c_cbT", [128, NXT])
        c_dtb = sb("c_dtb", [NH, 1])
        c_alog = sb("c_alog", [NH, 1])
        c_expa = sb("c_expa", [NH, 1])
        c_dbc = sb("c_dbc", [128, NH])
        c_nwT = sb("c_nwT", [128, 16])
        c_diag = sb("c_diag", [128, NXT * 4 * 128], BF16)
        c_wr = sb("c_wr", [128, 64], BF16)
        c_br = sb("c_br", [1, E], BF16)
        c_b2 = sb("c_b2", [E, D])
        c_eps = sb("c_eps", [128, 1])
        c_one = sb("c_one", [128, 1])
        c_b1 = sb("c_b1", [128, E * 32])

        with nc.Block() as blk:
            @blk.sync
            def _(y):
                for dst, src in [(c_u01, u01_h), (c_a1, a1s_h), (c_idf, idf_h),
                                 (c_ones, ones_h), (c_cbT, cbT_h), (c_dtb, dtb_h),
                                 (c_alog, alog_h), (c_dbc, dbc_h), (c_nwT, nwT_h),
                                 (c_b2, b2_h)]:
                    dX.inc(y.dma_start(dst[:, :], src[:, :]))
                dX.inc(y.dma_start(
                    bass.AP(c_b1, 0, [[E * 32, 128], [32, E], [1, 32]]),
                    bass.AP(b1T_h, 0, [[32, 128], [128 * 32, E], [1, 32]])))

            @blk.gpsimd
            def _(g):
                dW.inc(g.dma_start(c_a1b[:, :], a1s_h[:, :]))
                dW.inc(g.dma_start(c_idb[:, :], idf_h[:, :]))
                dW.inc(g.dma_start(c_onesb[:, :], ones_h[:, :]))
                dW.inc(g.dma_start(c_br[:, :], br_h[:, :]))
                dW.inc(g.dma_start(
                    bass.AP(c_wr, 0, [[64, 128], [8, 8], [1, 8]]),
                    bass.AP(wr_h, 0, [[8, 128], [128 * 8, 8], [1, 8]])))
                dW.inc(g.dma_start(
                    bass.AP(c_diag, 0, [[NXT * 4 * 128, 128], [128, NXT * 4], [1, 128]]),
                    bass.AP(cdiag_h, 0, [[128, 128], [128 * 128, NXT * 4], [1, 128]])))

            t_setup_x = dX.tok()
            t_setup_w = dW.tok()

            @blk.vector
            def _(v):
                v.memset(c_eps[:, :], 1e-5)
                cV.inc(v.memset(c_one[:, :], 1.0))

            @blk.scalar
            def _(s):
                W(s, "s", t_setup_x)
                cS.inc(s.activation(c_expa[:, :], c_alog[:, :], AF.Exp))

        m_ctx = ExitStack()
        with m_ctx:
            def msb(name, shape, dt=F32):
                return m_ctx.enter_context(nc.sbuf_tensor(name, shape, dt))

            m_wop = msb("m_wop", [128, 16 * D], BF16)
            m_xnt = msb("m_xnt", [128, 8 * SCL], BF16)
            m_xtm = msb("m_xtm", [128, NCH * D])
            m_xntm = msb("m_xntm", [128, NCH * D], BF16)
            m_z = msb("m_z", [128, NZT * SCL], BF16)
            m_xbc = msb("m_xbc", [128, NXT * XPW], BF16)
            m_dtraw = msb("m_dtraw", [NH, SCL])
            m_sp = msb("m_sp", [NH, SCL])
            m_t1 = msb("m_t1", [NH, SCL])
            m_dta = m_dtraw
            m_wip = msb("m_wip", [128, 4 * 1024], BF16)
            m_s = msb("m_s", [128, NH * HP])
            m_sbf = msb("m_sbf", [128, NH * HP], BF16)
            k_xs = msb("k_xs", [128, DIN], BF16)
            k_xd = msb("k_xd", [128, DIN], BF16)
            k_xdr = msb("k_xdr", [128, DIN], BF16)
            k_btm = msb("k_btm", [128, 128], BF16)
            k_g2u = msb("k_g2u", [128, 128], BF16)
            k_a2 = msb("k_a2", [128, 1024], BF16)
            k_m2 = msb("k_m2", [128, 2 * 1024], BF16)
            k_gm = msb("k_gm", [128, 2 * 1024], BF16)
            k_ytm = msb("k_ytm", [128, DIN])
            k_yrms = msb("k_yrms", [128, DIN], BF16)
            k_yrt = msb("k_yrt", [128, DIN], BF16)
            k_dtat = msb("k_dtat", [128, NH])
            k_dtt = msb("k_dtt", [128, NH])
            k_cumt = msb("k_cumt", [128, NH])
            k_bcpq = msb("k_bcpq", [128, NH])
            k_r = msb("k_r", [128, NH])
            k_pt = msb("k_pt", [128, NH])
            k_rarg = msb("k_rarg", [128, NH])
            k_cumq = msb("k_cumq", [NH, 1])
            k_cumqr = msb("k_cumqr", [1, NH])
            k_pqr = msb("k_pqr", [1, NH])
            k_ln = msb("k_ln", [128, 24])
            k_ssq = msb("k_ssq", [128, 1])
            k_rstd = msb("k_rstd", [128, 1])
            b_base = msb("b_base", [128, 2 * D])

            p0 = m_ctx.enter_context(nc.psum_tensor("p0", [128, 512], F32))
            p1 = m_ctx.enter_context(nc.psum_tensor("p1", [128, 512], F32))
            pT1 = m_ctx.enter_context(nc.psum_tensor("pT1", [128, 1024], BF16))
            pT2 = m_ctx.enter_context(nc.psum_tensor("pT2", [128, 1024], BF16))
            pE = m_ctx.enter_context(nc.psum_tensor("pE", [128, 512], F32))
            pF = m_ctx.enter_context(nc.psum_tensor("pF", [128, 512], F32))
            pG = m_ctx.enter_context(nc.psum_tensor("pG", [128, 512], F32))
            pM = m_ctx.enter_context(nc.psum_tensor("pM", [128, 512], F32))

            with nc.Block() as blk:
                @blk.gpsimd
                def _(g):
                    for kt in range(16):
                        dW.inc(g.dma_start(
                            m_wop[:, kt * D:(kt + 1) * D],
                            bass.AP(wopT_h, kt * 128 * D, [[D, 128], [1, D]])))
                t_wop_dma = dW.tok()

                @blk.vector
                def _(v):
                    W(v, "v", t_wop_dma)
                    for kt in range(16):
                        ins = v.tensor_scalar(m_wop[:, kt * D:(kt + 1) * D],
                                              m_wop[:, kt * D:(kt + 1) * D],
                                              c_nwT[:, kt:kt + 1], None, ALU.mult)
                    cV.inc(ins)
                t_wop = cV.tok()

                @blk.vector
                def _(v):
                    v.memset(m_s[:, :], 0.0)
                    v.memset(m_sbf[:, :], 0.0)
                    ins = v.memset(m_xbc[:, :], 0.0)
                    cV.inc(ins)
                t_minit = cV.tok()

                st = {
                    "xtm_free": None, "xntm_free": None, "xnt_free": None,
                    "z_free": None, "xbc_free": [None] * NXT,
                    "pT_free": [None, None], "pM_free": None,
                    "p01_free": [t_minit, t_minit],
                    "pE_free": t_minit, "pF_free": t_minit, "pG_free": t_minit,
                    "wip_slot": [None] * 4,
                    "a2_free": None, "m2_free": [None, None],
                    "gm_free": [None, None],
                    "state_w": t_minit, "base_free": [None, None],
                    "dt_free": None,
                }
                wip_f_global = [0]

                def emit_sc(sc):
                    @blk.sync
                    def _(y):
                        W(y, "y", st["xtm_free"])
                        t = dX.inc(y.dma_start(
                            m_xtm[:, :],
                            bass.AP(x_h, sc * SCL * D, [[D, 128], [128 * D, NCH], [1, D]])))
                        st["x_ld"] = t

                    @blk.vector
                    def _(v):
                        W(v, "v", st["x_ld"], st["xntm_free"], st.get("yrms_rd"))
                        for t4 in range(NCH):
                            xs_ = m_xtm[:, t4 * D:(t4 + 1) * D]
                            col = lambda j: k_ln[:, t4 * 6 + j:t4 * 6 + j + 1]
                            v.reduce_sum(col(0), xs_, axis=AXX)
                            v.scalar_tensor_tensor(k_yrms[:, 0:1024], xs_, 1.0, xs_,
                                                   ALU.mult, ALU.mult, accum_out=col(2))
                            v.drain()
                            v.tensor_scalar(col(1), col(0), -1.0 / D, None, ALU.mult)
                            v.drain()
                            v.tensor_tensor(col(3), col(1), col(1), ALU.mult)
                            v.drain()
                            ins = v.scalar_tensor_tensor(col(4), col(2), 1.0 / D, col(3),
                                                         ALU.mult, ALU.subtract)
                            cV.inc(ins)
                    t_var = cV.tok()

                    @blk.scalar
                    def _(s):
                        W(s, "s", t_var)
                        for t4 in range(NCH):
                            col = lambda j: k_ln[:, t4 * 6 + j:t4 * 6 + j + 1]
                            s.activation(col(5), col(4), AF.Ln, bias=c_eps[:, :])
                            s.drain()
                            ins = s.activation(col(5), col(5), AF.Exp, scale=-0.5)
                        cS.inc(ins)
                    t_rstd = cS.tok()

                    @blk.vector
                    def _(v):
                        W(v, "v", t_rstd)
                        for t4 in range(NCH):
                            xs_ = m_xtm[:, t4 * D:(t4 + 1) * D]
                            col = lambda j: k_ln[:, t4 * 6 + j:t4 * 6 + j + 1]
                            ins = v.tensor_scalar(m_xntm[:, t4 * D:(t4 + 1) * D],
                                                  xs_, col(1), col(5), ALU.add, ALU.mult)
                            cV.inc(ins)
                            st[("ln", t4)] = cV.tok()

                    for t4 in range(NCH):
                        pt = pT1 if t4 % 2 == 0 else pT2

                        @blk.tensor
                        def _(t, t4=t4, pt=pt):
                            W(t, "t", st[("ln", t4)], st["pT_free"][t4 % 2], st["xnt_free"])
                            for kd in range(8):
                                ins = t.transpose(
                                    pt[:, kd * 128:(kd + 1) * 128],
                                    m_xntm[:, t4 * D + kd * 128:t4 * D + (kd + 1) * 128],
                                    c_idb[:, :])
                            cT.inc(ins)
                        t_tp = cT.tok()

                        @blk.vector
                        def _(v, t4=t4, pt=pt, t_tp=t_tp):
                            W(v, "v", t_tp)
                            ins = v.tensor_copy(
                                bass.AP(m_xnt, t4 * 128, [[8 * SCL, 128], [SCL, 8], [1, 128]]),
                                bass.AP(pt, 0, [[1024, 128], [128, 8], [1, 128]]))
                            cV.inc(ins)
                        st["pT_free"][t4 % 2] = cV.tok()
                        st[("xnt", t4)] = cV.tok()

                    t_xnt_all = cV.tok()

                    @blk.scalar
                    def _(s):
                        W(s, "s", t_xnt_all)
                        t = dP.inc(s.dma_start(
                            bass.AP(xnt_hbm, sc * 128 * L, [[L, 128], [1, 8 * SCL]]),
                            m_xnt[:, :]))
                        st["spill"] = t

                    if "zx" in debug:
                        @blk.sync
                        def _(y):
                            W(y, "y", t_xnt_all)
                            dX.inc(y.dma_start(
                                bass.AP(dbg["xn"], sc * 128 * (NCH * D), [[NCH * D, 128], [1, NCH * D]]),
                                m_xntm[:, :]))

                    cT0_ip = cT.v
                    cS0_ip = cS.v

                    @blk.gpsimd
                    def _(g):
                        for f in range(35):
                            slot = f % 4
                            wd = 128 if f < 34 else 32
                            tok = st["wip_slot"][slot]
                            if f >= 4:
                                WF(g, "g", cT, cT0_ip + f - 4 + 1)
                            elif tok is not None:
                                W(g, "g", tok)
                            ins = g.dma_start(
                                bass.AP(m_wip, slot * 1024, [[4 * 1024, 128], [wd, 8], [1, wd]]),
                                bass.AP(wipT_h, f * 128, [[DIP, 128], [128 * DIP, 8], [1, wd]]))
                            st[("wipd", f)] = dW.inc(ins)
                            st["wip_slot"][slot] = (cT, cT0_ip + f + 1)
                            wip_f_global[0] += 1

                    @blk.tensor
                    def _(t):
                        W(t, "t", t_xnt_all, t_wop, st["p01_free"][0], st["p01_free"][1])
                        for f in range(35):
                            slot = f % 4
                            wd = 128 if f < 34 else 32
                            pp = p0 if f % 2 == 0 else p1
                            W(t, "t", st[("wipd", f)])
                            if f >= 2:
                                WF(t, "t", cS, cS0_ip + f - 2 + 1)
                            for kd in range(8):
                                ins = t.matmul(pp[:wd, :],
                                               m_wip[:, slot * 1024 + kd * wd:slot * 1024 + (kd + 1) * wd],
                                               m_xnt[:, kd * SCL:(kd + 1) * SCL],
                                               start=(kd == 0), stop=(kd == 7))
                            cT.inc(ins)

                    @blk.scalar
                    def _(s):
                        for f in range(35):
                            pp = p0 if f % 2 == 0 else p1
                            WF(s, "s", cT, cT0_ip + f + 1)
                            if f < NZT:
                                ins = s.activation(m_z[:, f * SCL:(f + 1) * SCL], pp[:, :], AF.Silu)
                            elif f < 34:
                                ct = f - NZT
                                ins = s.activation(
                                    bass.AP(m_xbc, ct * XPW + 4, [[NXT * XPW, 128], [1, SCL]]),
                                    pp[:, :], AF.Copy)
                            else:
                                ins = s.activation(m_dtraw[:, :], pp[:NH, :], AF.Copy)
                            cS.inc(ins)
                            st[("ipev", f)] = cS.tok()
                    st["p01_free"][0] = (cS, cS0_ip + 35)
                    st["p01_free"][1] = (cS, cS0_ip + 35)

                    @blk.scalar
                    def _(s):
                        W(s, "s", st[("ipev", 34)])
                        s.activation(m_t1[:, :], m_dtraw[:, :], AF.Abs, bias=c_dtb[:, :])
                        s.drain()
                        s.activation(m_t1[:, :], m_t1[:, :], AF.Exp, scale=-1.0)
                        s.drain()
                        s.activation(m_t1[:, :], m_t1[:, :], AF.Ln, bias=c_one[0:NH, :])
                        s.drain()
                        ins = s.activation(m_sp[:, :], m_dtraw[:, :], AF.Relu, bias=c_dtb[:, :])
                        cS.inc(ins)
                    t_sp1 = cS.tok()

                    @blk.vector
                    def _(v):
                        W(v, "v", t_sp1, t_setup_x)
                        v.tensor_tensor(m_sp[:, :], m_sp[:, :], m_t1[:, :], ALU.add)
                        ins = v.tensor_scalar(m_dta[:, :], m_sp[:, :], c_expa[:, :], -1.0,
                                              ALU.mult, ALU.mult)
                        cV.inc(ins)
                    st["dtA"] = cV.tok()
                    st["dt_free"] = None

                    cT0_cv = cT.v
                    cS0_cv = cS.v

                    @blk.tensor
                    def _(t):
                        W(t, "t", t_setup_w, st["p01_free"][0], st["p01_free"][1])
                        for ct in range(NXT):
                            pp = p0 if ct % 2 == 0 else p1
                            W(t, "t", st[("ipev", 16 + ct)])
                            if ct >= 2:
                                WF(t, "t", cS, cS0_cv + ct - 2 + 1)
                            for k in range(4):
                                ins = t.matmul(
                                    pp[:, :],
                                    c_diag[:, (ct * 4 + k) * 128:(ct * 4 + k + 1) * 128],
                                    bass.AP(m_xbc, ct * XPW + 1 + k, [[NXT * XPW, 128], [1, SCL]]),
                                    start=(k == 0), stop=(k == 3))
                            cT.inc(ins)

                    @blk.vector
                    def _(v):
                        for ct in range(NXT):
                            WF(v, "v", cT, cT0_cv + ct + 1)
                            ins = v.tensor_copy(
                                bass.AP(m_xbc, ct * XPW + 1, [[NXT * XPW, 128], [1, 3]]),
                                bass.AP(m_xbc, ct * XPW + 513, [[NXT * XPW, 128], [1, 3]]))
                            cV.inc(ins)
                    cV0_cv = cV.v

                    @blk.scalar
                    def _(s):
                        for ct in range(NXT):
                            pp = p0 if ct % 2 == 0 else p1
                            WF(s, "s", cV, cV0_cv - NXT + ct + 1)
                            ins = s.activation(
                                bass.AP(m_xbc, ct * XPW + 4, [[NXT * XPW, 128], [1, SCL]]),
                                pp[:, :], AF.Silu, bias=c_cbT[:, ct:ct + 1])
                            cS.inc(ins)
                            st[("cv", ct)] = cS.tok()
                    st["p01_free"][0] = (cS, cS0_cv + NXT)
                    st["p01_free"][1] = (cS, cS0_cv + NXT)

                    t_conv_all = cS.tok()

                    if "zx" in debug:
                        @blk.sync
                        def _(y):
                            W(y, "y", st[("ipev", 15)], t_conv_all, st["dtA"])
                            dX.inc(y.dma_start(
                                bass.AP(dbg["z"], sc * 128 * (NZT * SCL), [[NZT * SCL, 128], [1, NZT * SCL]]),
                                m_z[:, :]))
                            dX.inc(y.dma_start(
                                bass.AP(dbg["xbc"], sc * 128 * (NXT * XPW), [[NXT * XPW, 128], [1, NXT * XPW]]),
                                m_xbc[:, :]))
                            dX.inc(y.dma_start(
                                bass.AP(dbg["dt"], sc * NH * SCL, [[SCL, NH], [1, SCL]]),
                                m_dtraw[:, :]))
                            dX.inc(y.dma_start(
                                bass.AP(dbg["sp"], sc * NH * SCL, [[SCL, NH], [1, SCL]]),
                                m_sp[:, :]))

                    for c in range(NCH):
                        emit_chunk(sc, c)

                    st["xtm_free"] = cV.tok()
                    st["xntm_free"] = cV.tok()
                    st["xnt_free"] = st["spill"]
                    st["z_free"] = cV.tok()
                    for ct in range(NXT):
                        st["xbc_free"][ct] = cV.tok()
                    st["dt_free"] = cV.tok()

                def emit_chunk(sc, c):
                    co = c * Q
                    @blk.vector
                    def _(v):
                        W(v, "v", st["dtA"])
                        ins = v.reduce_sum(k_cumq[:, :], m_dta[:, co:co + Q], axis=AXX)
                        cV.inc(ins)
                    t_cumq = cV.tok()

                    @blk.tensor
                    def _(t):
                        W(t, "t", t_cumq, st["pM_free"])
                        t.transpose(pM[:, 0:32], m_dta[:NH, co:co + Q], c_idf[0:NH, 0:NH])
                        t.transpose(pM[:, 32:64], m_sp[:NH, co:co + Q], c_idf[0:NH, 0:NH])
                        ins = t.transpose(pM[0:1, 64:96], k_cumq[:, :], c_idf[0:NH, 0:NH])
                        cT.inc(ins)
                    t_rA = cT.tok()

                    @blk.vector
                    def _(v):
                        W(v, "v", t_rA)
                        v.tensor_copy(k_dtat[:, :], pM[:, 0:32])
                        v.tensor_copy(k_dtt[:, :], pM[:, 32:64])
                        ins = v.tensor_copy(k_cumqr[:, :], pM[0:1, 64:96])
                        cV.inc(ins)
                    t_rAev = cV.tok()

                    @blk.scalar
                    def _(s):
                        W(s, "s", t_rAev)
                        ins = s.activation(k_pqr[:, :], k_cumqr[:, :], AF.Exp)
                        cS.inc(ins)
                    t_pqr = cS.tok()

                    @blk.tensor
                    def _(t):
                        W(t, "t", t_rAev, t_pqr)
                        t.matmul(pM[:, 0:32], c_u01[:, :], k_dtat[:, :], start=True, stop=True)
                        t.matmul(pM[:, 32:64], c_ones[:, :], k_cumqr[:, :], start=True, stop=True)
                        ins = t.matmul(pM[:, 64:96], c_ones[:, :], k_pqr[:, :], start=True, stop=True)
                        cT.inc(ins)
                    t_rB = cT.tok()

                    @blk.vector
                    def _(v):
                        W(v, "v", t_rB)
                        v.tensor_copy(k_cumt[:, :], pM[:, 0:32])
                        v.drain()
                        v.tensor_tensor(k_rarg[:, :], pM[:, 32:64], k_cumt[:, :], ALU.subtract)
                        ins = v.tensor_copy(k_bcpq[:, :], pM[:, 64:96])
                        cV.inc(ins)
                    st["pM_free"] = cV.tok()
                    t_rBev = cV.tok()

                    @blk.scalar
                    def _(s):
                        W(s, "s", t_rBev)
                        s.activation(k_pt[:, :], k_cumt[:, :], AF.Exp)
                        ins = s.activation(k_r[:, :], k_rarg[:, :], AF.Exp)
                        cS.inc(ins)
                    t_ptr = cS.tok()

                    if "ssd" in debug:
                        @blk.sync
                        def _(y):
                            W(y, "y", t_rBev)
                            dX.inc(y.dma_start(
                                bass.AP(dbg["cumt"], (sc * NCH + c) * 128 * NH, [[NH, 128], [1, NH]]),
                                k_cumt[:, :]))

                    @blk.tensor
                    def _(t):
                        W(t, "t", st[("cv", 16)], st[("cv", 0)], st[("cv", 6)],
                          st["pT_free"][0], st["pT_free"][1])
                        for j in range(7):
                            W(t, "t", st[("cv", j)])
                        t.transpose(pT1[:, 0:128],
                                    bass.AP(m_xbc, 16 * XPW + 4 + co, [[NXT * XPW, 128], [1, Q]]),
                                    c_idb[:, :])
                        for j in range(7):
                            ins = t.transpose(pT1[:, (j + 1) * 128:(j + 2) * 128],
                                              bass.AP(m_xbc, j * XPW + 4 + co, [[NXT * XPW, 128], [1, Q]]),
                                              c_idb[:, :])
                        cT.inc(ins)
                    t_tpa = cT.tok()

                    @blk.tensor
                    def _(t):
                        for j in range(7, 15):
                            W(t, "t", st[("cv", j)])
                        for j in range(7, 15):
                            ins = t.transpose(pT2[:, (j - 7) * 128:(j - 6) * 128],
                                              bass.AP(m_xbc, j * XPW + 4 + co, [[NXT * XPW, 128], [1, Q]]),
                                              c_idb[:, :])
                        cT.inc(ins)
                    t_tpb = cT.tok()

                    @blk.vector
                    def _(v):
                        W(v, "v", t_tpa, t_tpb)
                        v.tensor_copy(k_btm[:, :], pT1[:, 0:128])
                        v.tensor_copy(k_xs[:, 0:7 * 128], pT1[:, 128:1024])
                        ins = v.tensor_copy(k_xs[:, 7 * 128:15 * 128], pT2[:, :])
                        cV.inc(ins)
                    t_ev1 = cV.tok()
                    st["pT_free"][0] = t_ev1
                    st["pT_free"][1] = t_ev1

                    @blk.tensor
                    def _(t):
                        W(t, "t", st[("cv", 15)], t_ev1)
                        ins = t.transpose(pT1[:, 0:128],
                                          bass.AP(m_xbc, 15 * XPW + 4 + co, [[NXT * XPW, 128], [1, Q]]),
                                          c_idb[:, :])
                        cT.inc(ins)
                    t_tpc = cT.tok()

                    @blk.vector
                    def _(v):
                        W(v, "v", t_tpc)
                        ins = v.tensor_copy(k_xs[:, 15 * 128:DIN], pT1[:, 0:128])
                        cV.inc(ins)
                    st["pT_free"][0] = cV.tok()
                    t_xs = cV.tok()

                    @blk.vector
                    def _(v):
                        W(v, "v", t_xs, t_ptr)
                        for h in range(NH):
                            v.tensor_scalar(k_xd[:, h * HP:(h + 1) * HP],
                                            k_xs[:, h * HP:(h + 1) * HP],
                                            k_dtt[:, h:h + 1], None, ALU.mult)
                        for h in range(NH):
                            ins = v.tensor_scalar(k_xdr[:, h * HP:(h + 1) * HP],
                                                  k_xd[:, h * HP:(h + 1) * HP],
                                                  k_r[:, h:h + 1], None, ALU.mult)
                        cV.inc(ins)
                    t_xdr = cV.tok()

                    @blk.tensor
                    def _(t):
                        W(t, "t", st[("cv", 16)], st[("cv", 17)], st["pE_free"])
                        ins = t.matmul(pE[:, 0:128],
                                       bass.AP(m_xbc, 16 * XPW + 4 + co, [[NXT * XPW, 128], [1, Q]]),
                                       bass.AP(m_xbc, 17 * XPW + 4 + co, [[NXT * XPW, 128], [1, Q]]),
                                       start=True, stop=True)
                        cT.inc(ins)
                    t_g2 = cT.tok()

                    @blk.vector
                    def _(v):
                        W(v, "v", t_g2)
                        ins = v.tensor_tensor(k_g2u[:, :], pE[:, 0:128], c_u01[:, :], ALU.mult)
                        cV.inc(ins)
                    t_g2u = cV.tok()
                    st["pE_free"] = t_g2u

                    for g in range(4):
                        gb = g % 2
                        hh0 = g * 8

                        @blk.vector
                        def _(v, g=g, gb=gb, hh0=hh0):
                            W(v, "v", t_rAev, st["a2_free"])
                            for i in range(8):
                                ins = v.tensor_scalar(k_a2[:, i * 128:(i + 1) * 128],
                                                      c_u01[:, :], k_dtat[:, hh0 + i:hh0 + i + 1],
                                                      None, ALU.mult)
                            cV.inc(ins)
                        t_a2 = cV.tok()

                        @blk.tensor
                        def _(t, g=g, gb=gb, t_a2=t_a2):
                            W(t, "t", t_a2, st["p01_free"][0], st["p01_free"][1])
                            for i in range(8):
                                pp = p0 if i < 4 else p1
                                ins = t.matmul(pp[:, (i % 4) * 128:(i % 4 + 1) * 128],
                                               c_a1b[:, :],
                                               k_a2[:, i * 128:(i + 1) * 128],
                                               start=True, stop=True)
                            cT.inc(ins)
                        t_d2 = cT.tok()
                        st["a2_free"] = t_d2

                        @blk.scalar
                        def _(s, gb=gb, t_d2=t_d2):
                            W(s, "s", t_d2, st["m2_free"][gb])
                            s.activation(k_m2[:, gb * 1024:gb * 1024 + 512], p0[:, :], AF.Exp)
                            ins = s.activation(k_m2[:, gb * 1024 + 512:gb * 1024 + 1024], p1[:, :], AF.Exp)
                            cS.inc(ins)
                        t_m2 = cS.tok()
                        st["p01_free"][0] = t_m2
                        st["p01_free"][1] = t_m2

                        @blk.vector
                        def _(v, gb=gb, t_m2=t_m2):
                            W(v, "v", t_m2, t_g2u, st["gm_free"][gb])
                            for i in range(8):
                                ins = v.tensor_tensor(
                                    k_gm[:, gb * 1024 + i * 128:gb * 1024 + (i + 1) * 128],
                                    k_g2u[:, :],
                                    k_m2[:, gb * 1024 + i * 128:gb * 1024 + (i + 1) * 128],
                                    ALU.mult)
                            cV.inc(ins)
                        t_gm = cV.tok()
                        st["m2_free"][gb] = t_gm

                        @blk.tensor
                        def _(t, g=g, gb=gb, hh0=hh0, t_gm=t_gm):
                            W(t, "t", t_gm, t_xdr, st["state_w"], st.get(("sbf_prev", g)),
                              st["pE_free"], st["pF_free"], st["pG_free"])
                            for i in range(8):
                                h = hh0 + i
                                t.matmul(pE[:, i * HP:(i + 1) * HP],
                                         k_gm[:, gb * 1024 + i * 128:gb * 1024 + (i + 1) * 128],
                                         k_xd[:, h * HP:(h + 1) * HP], start=True, stop=True)
                            for i in range(8):
                                h = hh0 + i
                                t.matmul(pF[:, i * HP:(i + 1) * HP],
                                         bass.AP(m_xbc, 17 * XPW + 4 + co, [[NXT * XPW, 128], [1, Q]]),
                                         m_sbf[:, h * HP:(h + 1) * HP], start=True, stop=True)
                            for i in range(8):
                                h = hh0 + i
                                ins = t.matmul(pG[:, i * HP:(i + 1) * HP],
                                               k_btm[:, :],
                                               k_xdr[:, h * HP:(h + 1) * HP], start=True, stop=True)
                            cT.inc(ins)
                        t_y = cT.tok()
                        st["gm_free"][gb] = t_y

                        @blk.vector
                        def _(v, g=g, hh0=hh0, t_y=t_y):
                            W(v, "v", t_y, t_ptr, st.get(("sbf_prev", g)))
                            v.tensor_copy(k_ytm[:, g * 512:(g + 1) * 512], pE[:, :])
                            for i in range(8):
                                h = hh0 + i
                                sl = k_ytm[:, h * HP:(h + 1) * HP]
                                v.scalar_tensor_tensor(sl, pF[:, i * HP:(i + 1) * HP],
                                                       k_pt[:, h:h + 1], sl, ALU.mult, ALU.add)
                            for i in range(8):
                                h = hh0 + i
                                sl = k_ytm[:, h * HP:(h + 1) * HP]
                                v.scalar_tensor_tensor(sl, k_xs[:, h * HP:(h + 1) * HP],
                                                       c_dbc[:, h:h + 1], sl, ALU.mult, ALU.add)
                            for i in range(8):
                                h = hh0 + i
                                sl = m_s[:, h * HP:(h + 1) * HP]
                                v.scalar_tensor_tensor(sl, sl, k_bcpq[:, h:h + 1],
                                                       pG[:, i * HP:(i + 1) * HP],
                                                       ALU.mult, ALU.add)
                            for i in range(8):
                                h = hh0 + i
                                ins = v.tensor_copy(m_sbf[:, h * HP:(h + 1) * HP],
                                                    m_s[:, h * HP:(h + 1) * HP])
                            cV.inc(ins)
                        t_yev = cV.tok()
                        st[("sbf", g)] = t_yev
                        st["pE_free"] = t_yev
                        st["pF_free"] = t_yev
                        st["pG_free"] = t_yev
                    st["state_w"] = cV.tok()
                    for g in range(4):
                        st[("sbf_prev", g)] = st.get(("sbf", g))

                    @blk.tensor
                    def _(t):
                        W(t, "t", st[("ipev", 15)], st["pT_free"][0], st["pT_free"][1],
                          st["state_w"])
                        for j in range(8):
                            t.transpose(pT1[:, j * 128:(j + 1) * 128],
                                        m_z[:, j * SCL + co:j * SCL + co + Q], c_idb[:, :])
                        for j in range(8, 16):
                            ins = t.transpose(pT2[:, (j - 8) * 128:(j - 7) * 128],
                                              m_z[:, j * SCL + co:j * SCL + co + Q], c_idb[:, :])
                        cT.inc(ins)
                    t_ztp = cT.tok()

                    @blk.vector
                    def _(v):
                        W(v, "v", t_ztp)
                        v.tensor_tensor(k_ytm[:, 0:1024], k_ytm[:, 0:1024], pT1[:, :], ALU.mult)
                        ins = v.tensor_tensor(k_ytm[:, 1024:2048], k_ytm[:, 1024:2048], pT2[:, :], ALU.mult)
                        cV.inc(ins)
                    t_gate = cV.tok()
                    st["pT_free"][0] = t_gate
                    st["pT_free"][1] = t_gate

                    if "ssd" in debug:
                        @blk.sync
                        def _(y):
                            W(y, "y", t_gate)
                            dX.inc(y.dma_start(
                                bass.AP(dbg["ytm"], (sc * NCH + c) * 128 * DIN, [[DIN, 128], [1, DIN]]),
                                k_ytm[:, :]))

                    @blk.vector
                    def _(v):
                        W(v, "v", t_gate)
                        ins = v.scalar_tensor_tensor(k_yrms[:, :], k_ytm[:, :], 1.0, k_ytm[:, :],
                                                     ALU.mult, ALU.mult, accum_out=k_ssq[:, :])
                        cV.inc(ins)
                    t_ssq = cV.tok()

                    @blk.scalar
                    def _(s):
                        W(s, "s", t_ssq)
                        s.activation(k_rstd[:, :], k_ssq[:, :], AF.Ln, bias=c_eps[:, :], scale=1.0 / DIN)
                        s.drain()
                        ins = s.activation(k_rstd[:, :], k_rstd[:, :], AF.Exp, scale=-0.5)
                        cS.inc(ins)
                    t_rstd = cS.tok()

                    @blk.vector
                    def _(v):
                        W(v, "v", t_rstd)
                        ins = v.tensor_scalar(k_yrms[:, :], k_ytm[:, :], k_rstd[:, :], None, ALU.mult)
                        cV.inc(ins)
                    t_yrms = cV.tok()

                    if "ssd" in debug:
                        @blk.sync
                        def _(y):
                            W(y, "y", t_yrms)
                            dX.inc(y.dma_start(
                                bass.AP(dbg["yrms"], (sc * NCH + c) * 128 * DIN, [[DIN, 128], [1, DIN]]),
                                k_yrms[:, :]))

                    @blk.tensor
                    def _(t):
                        W(t, "t", t_yrms, st["pT_free"][0], st["pT_free"][1])
                        for kt in range(8):
                            t.transpose(pT1[:, kt * 128:(kt + 1) * 128],
                                        k_yrms[:, kt * 128:(kt + 1) * 128], c_idb[:, :])
                        for kt in range(8, 16):
                            ins = t.transpose(pT2[:, (kt - 8) * 128:(kt - 7) * 128],
                                              k_yrms[:, kt * 128:(kt + 1) * 128], c_idb[:, :])
                        cT.inc(ins)
                    t_ytp = cT.tok()
                    st["yrms_rd"] = t_ytp

                    @blk.vector
                    def _(v):
                        W(v, "v", t_ytp)
                        v.tensor_copy(k_yrt[:, 0:1024], pT1[:, :])
                        ins = v.tensor_copy(k_yrt[:, 1024:2048], pT2[:, :])
                        cV.inc(ins)
                    t_yrt = cV.tok()
                    st["pT_free"][0] = t_yrt
                    st["pT_free"][1] = t_yrt

                    @blk.tensor
                    def _(t):
                        W(t, "t", t_yrt, st["pE_free"], st["pF_free"], t_wop)
                        for kt in range(16):
                            t.matmul(pE[:, :], k_yrt[:, kt * 128:(kt + 1) * 128],
                                     m_wop[:, kt * D:kt * D + 512],
                                     start=(kt == 0), stop=(kt == 15))
                        for kt in range(16):
                            ins = t.matmul(pF[:, :], k_yrt[:, kt * 128:(kt + 1) * 128],
                                           m_wop[:, kt * D + 512:(kt + 1) * D],
                                           start=(kt == 0), stop=(kt == 15))
                        cT.inc(ins)
                    t_op = cT.tok()

                    bslot = c % 2

                    @blk.vector
                    def _(v, c=c, bslot=bslot, t_op=t_op):
                        W(v, "v", t_op, st["base_free"][bslot])
                        bb = b_base[:, bslot * D:(bslot + 1) * D]
                        v.tensor_tensor(bb, m_xtm[:, c * D:(c + 1) * D],
                                        m_xntm[:, c * D:(c + 1) * D], ALU.add)
                        v.tensor_tensor(b_base[:, bslot * D:bslot * D + 512],
                                        b_base[:, bslot * D:bslot * D + 512], pE[:, :], ALU.add)
                        ins = v.tensor_tensor(b_base[:, bslot * D + 512:(bslot + 1) * D],
                                              b_base[:, bslot * D + 512:(bslot + 1) * D], pF[:, :], ALU.add)
                        cV.inc(ins)
                    t_base = cV.tok()
                    st["pE_free"] = t_base
                    st["pF_free"] = t_base

                    @blk.sync
                    def _(y, sc=sc, c=c, bslot=bslot, t_base=t_base):
                        W(y, "y", t_base)
                        ins = y.dma_start(
                            bass.AP(out_h, (sc * SCL + c * Q) * D, [[D, 128], [1, D]]),
                            b_base[:, bslot * D:(bslot + 1) * D])
                        st["base_free"][bslot] = dX.inc(ins)

                for sc in range(n_sc):
                    emit_sc(sc)

                t_base_all = dX.tok()
                t_spill_all = dP.tok()

                @blk.sync
                def _(y):
                    W(y, "y", t_base_all)

        if do_ffn:
            f_ctx = ExitStack()
            with f_ctx:
                def fsb(name, shape, dt=F32):
                    return f_ctx.enter_context(nc.sbuf_tensor(name, shape, dt))

                f_xnt = fsb("f_xnt", [128, 2 * L], BF16)
                f_ht = fsb("f_ht", [128, 64 * 512], BF16)
                f_acc = fsb("f_acc", [128, 2 * 8 * D])
                f_w1 = fsb("f_w1", [128, 4 * 1024], BF16)
                f_w2 = fsb("f_w2", [128, 8 * 512], BF16)
                f_lg = fsb("f_lg", [128, 64])
                f_w = fsb("f_w", [128, 256])
                f_wt = fsb("f_wt", [8, L // 128 * 128])
                f_sm = fsb("f_sm", [128, 64])
                f_s8a = fsb("f_s8a", [128, 64])
                f_s8b = fsb("f_s8b", [128, 64])

                q = [f_ctx.enter_context(nc.psum_tensor(f"q{i}", [128, 512], F32))
                     for i in range(8)]

                with nc.Block() as blk:
                    fst = {
                        "q_free": [None] * 8,
                        "w1_slot": [None] * 4,
                        "w2_slot": [None] * 8,
                        "acc_free": [None, None],
                        "xnt_free": None,
                        "ht_tok": None,
                    }

                    def emit_pair(p):
                        aslot = p % 2
                        @blk.scalar
                        def _(s, p=p):
                            W(s, "s", t_spill_all, fst["xnt_free"])
                            for sub01 in range(2):
                                sub = p * 2 + sub01
                                ins = s.dma_start(
                                    f_xnt[:, sub01 * L:sub01 * L + L],
                                    bass.AP(xnt_hbm, sub * 128 * L, [[L, 128], [1, L]]))
                                fst[("xntd", sub01)] = dP.inc(ins)

                        @blk.sync
                        def _(y, p=p, aslot=aslot):
                            W(y, "y", t_base_all, fst["acc_free"][aslot])
                            ins = y.dma_start(
                                f_acc[:, aslot * 8 * D:(aslot + 1) * 8 * D],
                                bass.AP(out_h, p * 1024 * D, [[D, 128], [128 * D, 8], [1, D]]))
                            fst["base_ld"] = dX.inc(ins)

                        @blk.tensor
                        def _(t, p=p):
                            W(t, "t", fst[("xntd", 0)], fst[("xntd", 1)], t_setup_w, t_setup_x)
                            W(t, "t", fst["q_free"][7])
                            for i in range(8):
                                sub01, t128 = i // 4, i % 4
                                t.matmul(q[7][:, i * 8:(i + 1) * 8], c_onesb[:, :], c_br[:, :],
                                         start=True, stop=False)
                                for kd in range(8):
                                    ins = t.matmul(
                                        q[7][:, i * 8:(i + 1) * 8],
                                        f_xnt[:, sub01 * L + kd * 512 + t128 * 128:
                                              sub01 * L + kd * 512 + (t128 + 1) * 128],
                                        c_wr[:, kd * 8:(kd + 1) * 8],
                                        start=False, stop=(kd == 7))
                            cT.inc(ins)
                        t_lg = cT.tok()

                        @blk.vector
                        def _(v, t_lg=t_lg):
                            W(v, "v", t_lg)
                            ins = v.tensor_copy(f_lg[:, :], q[7][:, 0:64])
                            cV.inc(ins)
                        t_lgev = cV.tok()
                        fst["q_free"][7] = t_lgev

                        sl = lambda i, j: f_sm[:, i * 8 + j:i * 8 + j + 1]
                        lgs = lambda i: f_lg[:, i * 8:(i + 1) * 8]
                        s8a = lambda i: f_s8a[:, i * 8:(i + 1) * 8]
                        s8b = lambda i: f_s8b[:, i * 8:(i + 1) * 8]

                        @blk.vector
                        def _(v):
                            W(v, "v", t_lgev)
                            for i in range(8):
                                v.reduce_max(sl(i, 0), lgs(i), axis=AXX)
                            for i in range(8):
                                v.tensor_scalar(sl(i, 1), sl(i, 0), -1.0, None, ALU.mult)
                            for i in range(8):
                                v.tensor_scalar(s8a(i), lgs(i), sl(i, 0), None, ALU.is_ge)
                            for i in range(8):
                                v.scalar_tensor_tensor(s8b(i), s8a(i), -1e30, lgs(i),
                                                       ALU.mult, ALU.add)
                            for i in range(8):
                                ins = v.reduce_max(sl(i, 2), s8b(i), axis=AXX)
                            cV.inc(ins)
                        t_m2x = cV.tok()

                        @blk.scalar
                        def _(s):
                            W(s, "s", t_m2x)
                            for i in range(8):
                                ins = s.activation(s8a(i), lgs(i), AF.Exp, bias=sl(i, 1))
                            cS.inc(ins)
                        t_expv = cS.tok()

                        @blk.vector
                        def _(v, p=p):
                            W(v, "v", t_expv)
                            for i in range(8):
                                v.tensor_scalar(s8b(i), lgs(i), sl(i, 2), None, ALU.is_ge)
                            for i in range(8):
                                v.tensor_tensor(s8a(i), s8a(i), s8b(i), ALU.mult)
                            for i in range(8):
                                v.reduce_sum(sl(i, 3), s8a(i), axis=AXX)
                            for i in range(8):
                                v.reciprocal(sl(i, 4), sl(i, 3))
                            for i in range(8):
                                ins = v.tensor_scalar(f_w[:, (p * 8 + i) * 8:(p * 8 + i + 1) * 8],
                                                      s8a(i), sl(i, 4), None, ALU.mult)
                            cV.inc(ins)
                        t_wall = cV.tok()
                        for i in range(8):
                            fst[("wrt", i)] = t_wall

                        @blk.tensor
                        def _(t, p=p):
                            W(t, "t", fst["q_free"][6], fst["q_free"][5])
                            for i in range(8):
                                W(t, "t", fst[("wrt", i)])
                                qq = q[6] if i < 4 else q[5]
                                ins = t.transpose(qq[0:8, (i % 4) * 128:(i % 4 + 1) * 128],
                                                  f_w[:, (p * 8 + i) * 8:(p * 8 + i + 1) * 8],
                                                  c_idf[:, :])
                            cT.inc(ins)
                        t_wtp = cT.tok()

                        @blk.vector
                        def _(v, p=p, t_wtp=t_wtp):
                            W(v, "v", t_wtp)
                            v.tensor_copy(f_wt[0:8, p * 1024:p * 1024 + 512], q[6][0:8, :])
                            ins = v.tensor_copy(f_wt[0:8, p * 1024 + 512:(p + 1) * 1024], q[5][0:8, :])
                            cV.inc(ins)
                        t_wtev = cV.tok()
                        fst["q_free"][6] = t_wtev
                        fst["q_free"][5] = t_wtev

                        for i in range(8):
                            @blk.tensor
                            def _(t, p=p, i=i):
                                W(t, "t", t_wtev, fst["q_free"][6], fst["q_free"][7])
                                t.matmul(q[6][:, :], f_wt[0:8, (p * 8 + i) * 128:(p * 8 + i + 1) * 128],
                                         c_b2[:, 0:512], start=True, stop=True)
                                ins = t.matmul(q[7][:, :], f_wt[0:8, (p * 8 + i) * 128:(p * 8 + i + 1) * 128],
                                               c_b2[:, 512:1024], start=True, stop=True)
                                cT.inc(ins)
                            t_b2 = cT.tok()

                            @blk.vector
                            def _(v, i=i, aslot=aslot, t_b2=t_b2):
                                W(v, "v", t_b2, fst["base_ld"])
                                a0 = aslot * 8 * D + i * D
                                v.tensor_tensor(f_acc[:, a0:a0 + 512], f_acc[:, a0:a0 + 512],
                                                q[6][:, :], ALU.add)
                                ins = v.tensor_tensor(f_acc[:, a0 + 512:a0 + 1024],
                                                      f_acc[:, a0 + 512:a0 + 1024], q[7][:, :], ALU.add)
                                cV.inc(ins)
                            tk = cV.tok()
                            fst["q_free"][6] = tk
                            fst["q_free"][7] = tk

                        for e in range(E):
                            emit_expert(p, e, aslot)

                        @blk.sync
                        def _(y, p=p, aslot=aslot):
                            W(y, "y", fst["evac_last"])
                            ins = y.dma_start(
                                bass.AP(out_h, p * 1024 * D, [[D, 128], [128 * D, 8], [1, D]]),
                                f_acc[:, aslot * 8 * D:(aslot + 1) * 8 * D])
                            fst["acc_free"][aslot] = dX.inc(ins)
                        fst["xnt_free"] = cT.tok()

                    def emit_expert(p, e, aslot):
                        cT0_a = cT.v
                        cS0_a = cS.v

                        @blk.gpsimd
                        def _(g, e=e):
                            for hs in range(32):
                                slot = hs % 4
                                if hs >= 4:
                                    WF(g, "g", cT, cT0_a + (hs - 4) * 2 + 2)
                                elif fst["w1_slot"][slot] is not None:
                                    W(g, "g", fst["w1_slot"][slot])
                                ins = g.dma_start(
                                    bass.AP(f_w1, slot * 1024, [[4 * 1024, 128], [128, 8], [1, 128]]),
                                    bass.AP(w1_h, (e * D) * HF + hs * 128,
                                            [[HF, 128], [128 * HF, 8], [1, 128]]))
                                fst[("w1d", hs)] = dW.inc(ins)
                                fst["w1_slot"][slot] = (cT, cT0_a + hs * 2 + 2)

                        @blk.tensor
                        def _(t):
                            for bkk in range(4):
                                W(t, "t", fst["q_free"][bkk])
                            for hs in range(32):
                                slot = hs % 4
                                W(t, "t", fst[("w1d", hs)])
                                for sub01 in range(2):
                                    bk = sub01 * 2 + hs % 2
                                    if hs >= 2:
                                        WF(t, "t", cS, cS0_a + (hs - 2) * 2 + sub01 + 1)
                                    for kd in range(8):
                                        ins = t.matmul(
                                            q[bk][:, :],
                                            f_w1[:, slot * 1024 + kd * 128:slot * 1024 + (kd + 1) * 128],
                                            f_xnt[:, sub01 * L + kd * 512:sub01 * L + (kd + 1) * 512],
                                            start=(kd == 0), stop=(kd == 7))
                                    cT.inc(ins)

                        @blk.scalar
                        def _(s, e=e):
                            for hs in range(32):
                                for sub01 in range(2):
                                    bk = sub01 * 2 + hs % 2
                                    WF(s, "s", cT, cT0_a + hs * 2 + sub01 + 1)
                                    ins = s.activation(
                                        f_ht[:, (sub01 * 32 + hs) * 512:(sub01 * 32 + hs + 1) * 512],
                                        q[bk][:, :], AF.Gelu,
                                        bias=c_b1[:, e * 32 + hs:e * 32 + hs + 1])
                                    cS.inc(ins)
                        t_ht_all = (cS, cS0_a + 64)
                        for bkk in range(8):
                            fst["q_free"][bkk] = t_ht_all

                        for dh in range(2):
                            for hs in range(32):
                                slot = (dh * 32 + hs) % 8

                                @blk.gpsimd
                                def _(g, e=e, dh=dh, hs=hs, slot=slot):
                                    W(g, "g", fst["w2_slot"][slot])
                                    ins = g.dma_start(
                                        bass.AP(f_w2, slot * 512, [[8 * 512, 128], [1, 512]]),
                                        bass.AP(w2_h, (e * HF + hs * 128) * D + dh * 512,
                                                [[D, 128], [1, 512]]))
                                    fst[("w2d", dh, hs)] = dW.inc(ins)

                                @blk.tensor
                                def _(t, dh=dh, hs=hs, slot=slot):
                                    W(t, "t", fst[("w2d", dh, hs)], t_ht_all)
                                    if hs == 0:
                                        for bkk in range(8):
                                            W(t, "t", fst["q_free"][bkk])
                                    for sub01 in range(2):
                                        for t128 in range(4):
                                            bk = sub01 * 4 + t128
                                            ins = t.matmul(
                                                q[bk][:, :],
                                                f_ht[:, (sub01 * 32 + hs) * 512 + t128 * 128:
                                                      (sub01 * 32 + hs) * 512 + (t128 + 1) * 128],
                                                f_w2[:, slot * 512:(slot + 1) * 512],
                                                start=(hs == 0), stop=(hs == 31))
                                    cT.inc(ins)
                                fst["w2_slot"][slot] = cT.tok()
                            t_bmm = cT.tok()

                            @blk.vector
                            def _(v, p=p, e=e, dh=dh, aslot=aslot, t_bmm=t_bmm):
                                W(v, "v", t_bmm)
                                for sub01 in range(2):
                                    for t128 in range(4):
                                        bk = sub01 * 4 + t128
                                        t128g = p * 8 + sub01 * 4 + t128
                                        a0 = aslot * 8 * D + (sub01 * 4 + t128) * D + dh * 512
                                        ins = v.scalar_tensor_tensor(
                                            f_acc[:, a0:a0 + 512], q[bk][:, :],
                                            f_w[:, t128g * 8 + e:t128g * 8 + e + 1],
                                            f_acc[:, a0:a0 + 512], ALU.mult, ALU.add)
                                        fst["q_free"][bk] = cV.inc(ins)
                            fst["evac_last"] = cV.tok()

                    for p in range(n_pairs):
                        emit_pair(p)

                    if "ffn" in debug:
                        @blk.sync
                        def _(y):
                            W(y, "y", cV.tok())
                            dX.inc(y.dma_start(dbg["lg"][:, 0:64], f_lg[:, :]))
                            dX.inc(y.dma_start(dbg["w"][:, :], f_w[:, :]))

                    @blk.sync
                    def _(y):
                        y.wait_ge(dX.sem, dX.v)

        else:
            with nc.Block() as blk:
                @blk.sync
                def _(y):
                    y.wait_ge(dX.sem, dX.v)

    return nc


_CACHE = {}


def _get_prog(key):
    if key not in _CACHE:
        n_sc, do_ffn, debug = key
        _CACHE[key] = build_program(n_sc=n_sc, do_ffn=do_ffn, debug=debug)
    return _CACHE[key]


def make_inputs(inputs):
    f32 = np.float32
    x = np.asarray(inputs["L0_attn_x"], f32)
    wipT = np.ascontiguousarray(np.asarray(inputs["L0_attn_in_proj_w"], f32).T)
    cw = np.asarray(inputs["L0_attn_conv_w"], f32)
    cdiag = np.zeros((NXT, 4, 128, 128), f32)
    ii = np.arange(128)
    cdiag[:, :, ii, ii] = cw.reshape(NXT, 128, 4).transpose(0, 2, 1)
    cdiag = cdiag.reshape(NXT * 4 * 128, 128)
    cbT = np.ascontiguousarray(np.asarray(inputs["L0_attn_conv_b"], f32).reshape(NXT, 128).T)
    dtb = np.ascontiguousarray(np.asarray(inputs["L0_attn_dt_bias"], f32).reshape(NH, 1))
    alog = np.ascontiguousarray(np.asarray(inputs["L0_attn_A_log"], f32).reshape(NH, 1))
    dbc = np.ascontiguousarray(np.broadcast_to(
        np.asarray(inputs["L0_attn_D"], f32).reshape(1, NH), (128, NH)))
    nwT = np.ascontiguousarray(np.asarray(inputs["L0_attn_norm_w"], f32).reshape(16, 128).T)
    wopT = np.ascontiguousarray(np.asarray(inputs["L0_attn_out_proj_w"], f32).T)
    w1 = np.ascontiguousarray(np.asarray(inputs["L0_ffn_W1"], f32).reshape(E * D, HF))
    b1T = np.ascontiguousarray(np.asarray(inputs["L0_ffn_b1"], f32)
                               .reshape(E, 32, 128).transpose(0, 2, 1)).reshape(E * 128, 32)
    w2 = np.ascontiguousarray(np.asarray(inputs["L0_ffn_W2"], f32).reshape(E * HF, D))
    b2 = np.ascontiguousarray(np.asarray(inputs["L0_ffn_b2"], f32))
    wr = np.ascontiguousarray(np.asarray(inputs["L0_ffn_Wr"], f32))
    br = np.ascontiguousarray(np.asarray(inputs["L0_ffn_br"], f32).reshape(1, E))

    s = np.arange(128)
    u01 = (s[:, None] <= s[None, :]).astype(f32)
    a1s = (s[:, None] > s[None, :]).astype(f32)
    idf = np.eye(128, dtype=f32)
    ones1 = np.ones((1, 128), f32)

    shared = dict(wipT=wipT, cdiag=cdiag, cbT=cbT, dtb=dtb, alog=alog, dbc=dbc,
                  nwT=nwT, wopT=wopT, w1=w1, b1T=b1T, w2=w2, b2=b2, wr=wr, br=br,
                  u01=u01, a1s=a1s, idf=idf, ones1=ones1)
    in_maps = []
    for b in range(8):
        d = dict(shared)
        d["x"] = np.ascontiguousarray(x[b])
        in_maps.append(d)
    return in_maps


def kernel(**inputs):
    nc = _get_prog((8, True, ()))
    in_maps = make_inputs(inputs)
    res = run_bass_kernel_spmd(nc, in_maps, core_ids=list(range(8)))
    return np.stack([np.asarray(r["out"], np.float32) for r in res.results], axis=0)


# version 06
# speedup vs baseline: 1.0832x; 1.0832x over previous
import numpy as np
import ml_dtypes
from contextlib import ExitStack
import concourse.bass as bass
import concourse.mybir as mybir
from concourse.bass_utils import run_bass_kernel_spmd

F32 = mybir.dt.float32
BF16 = mybir.dt.bfloat16
AF = mybir.ActivationFunctionType
ALU = mybir.AluOpType
AXX = mybir.AxisListType.X

L = 4096
D = 1024
DIN = 2048
NH = 32
HP = 64
NST = 128
DIP = 4384
NZT = 16
NXT = 18
XPW = 520
SCL = 512
NCH = 4
Q = 128
E = 8
HF = 4096


class Ch:
    def __init__(self, ctx, nc, name, step=1):
        self.sem = ctx.enter_context(nc.semaphore(name))
        self.v = 0
        self.step = step

    def inc(self, ins):
        self.v += self.step
        ins.then_inc(self.sem, self.step)
        return (self, self.v)

    def tok(self):
        return (self, self.v)


def build_program(n_sc=8, do_ffn=True, debug=(), n_pairs=4):
    nc = bass.Bass("TRN2", target_bir_lowering=False)

    x_h = nc.dram_tensor("x", [L, D], F32, kind="ExternalInput")
    wipT_h = nc.dram_tensor("wipT", [D, DIP], F32, kind="ExternalInput")
    cdiag_h = nc.dram_tensor("cdiag", [NXT * 4 * 128, 128], F32, kind="ExternalInput")
    cbT_h = nc.dram_tensor("cbT", [128, NXT], F32, kind="ExternalInput")
    dtb_h = nc.dram_tensor("dtb", [NH, 1], F32, kind="ExternalInput")
    alog_h = nc.dram_tensor("alog", [NH, 1], F32, kind="ExternalInput")
    dbc_h = nc.dram_tensor("dbc", [128, NH], F32, kind="ExternalInput")
    wopT_h = nc.dram_tensor("wopT", [DIN, D], F32, kind="ExternalInput")
    nwT_h = nc.dram_tensor("nwT", [128, 16], F32, kind="ExternalInput")
    w1_h = nc.dram_tensor("w1", [E * D, HF], F32, kind="ExternalInput")
    b1T_h = nc.dram_tensor("b1T", [E * 128, 32], F32, kind="ExternalInput")
    w2_h = nc.dram_tensor("w2", [E * HF, D], F32, kind="ExternalInput")
    b2_h = nc.dram_tensor("b2", [E, D], F32, kind="ExternalInput")
    wr_h = nc.dram_tensor("wr", [D, E], F32, kind="ExternalInput")
    br_h = nc.dram_tensor("br", [1, E], F32, kind="ExternalInput")
    u01_h = nc.dram_tensor("u01", [128, 128], F32, kind="ExternalInput")
    a1s_h = nc.dram_tensor("a1s", [128, 128], F32, kind="ExternalInput")
    idf_h = nc.dram_tensor("idf", [128, 128], F32, kind="ExternalInput")
    ones_h = nc.dram_tensor("ones1", [1, 128], F32, kind="ExternalInput")
    xnt_hbm = nc.dram_tensor("xnt_scr", [8 * 128, L], BF16)
    out_h = nc.dram_tensor("out", [L, D], F32, kind="ExternalOutput")

    dbg = {}
    if "zx" in debug:
        dbg["z"] = nc.dram_tensor("dbg_z", [8 * 128, NZT * SCL], BF16, kind="ExternalOutput")
        dbg["xbc"] = nc.dram_tensor("dbg_xbc", [8 * 128, NXT * XPW], BF16, kind="ExternalOutput")
        dbg["dt"] = nc.dram_tensor("dbg_dt", [8 * NH, SCL], F32, kind="ExternalOutput")
        dbg["sp"] = nc.dram_tensor("dbg_sp", [8 * NH, SCL], F32, kind="ExternalOutput")
        dbg["xn"] = nc.dram_tensor("dbg_xn", [8 * 128, NCH * D], BF16, kind="ExternalOutput")
    if "ssd" in debug:
        dbg["cumt"] = nc.dram_tensor("dbg_cumt", [L, NH], F32, kind="ExternalOutput")
        dbg["ytm"] = nc.dram_tensor("dbg_ytm", [L, DIN], F32, kind="ExternalOutput")
        dbg["yrms"] = nc.dram_tensor("dbg_yrms", [L, DIN], BF16, kind="ExternalOutput")
    if "ffn" in debug:
        dbg["lg"] = nc.dram_tensor("dbg_lg", [128, 256], F32, kind="ExternalOutput")
        dbg["w"] = nc.dram_tensor("dbg_w", [128, 256], F32, kind="ExternalOutput")

    ctx = ExitStack()
    with ctx:
        def sb(name, shape, dt=F32):
            return ctx.enter_context(nc.sbuf_tensor(name, shape, dt))

        def psum(name, shape, dt=F32):
            return ctx.enter_context(nc.psum_tensor(name, shape, dt))

        dSetS = Ch(ctx, nc, "dSetS", 16)
        dSetG = Ch(ctx, nc, "dSetG", 16)
        dWop = Ch(ctx, nc, "dWop", 16)
        dSpill = Ch(ctx, nc, "dSpill", 16)
        dXld = Ch(ctx, nc, "dXld", 16)
        dB = [Ch(ctx, nc, f"dB{i}", 16) for i in range(2)]
        dA = [Ch(ctx, nc, f"dA{i}", 16) for i in range(2)]
        dAL = Ch(ctx, nc, "dAL", 16)
        dXnt = Ch(ctx, nc, "dXnt", 16)
        dX = Ch(ctx, nc, "dX", 16)
        rWip = [Ch(ctx, nc, f"rWip{i}", 16) for i in range(4)]
        rW1 = [Ch(ctx, nc, f"rW1_{i}", 16) for i in range(4)]
        rW2 = [Ch(ctx, nc, f"rW2_{i}", 16) for i in range(8)]
        cT = Ch(ctx, nc, "cT")
        cV = Ch(ctx, nc, "cV")
        cS = Ch(ctx, nc, "cS")
        cG = Ch(ctx, nc, "cG")

        waited = {}

        def W(eng, key, *toks):
            hw = waited.setdefault(key, {})
            for t in toks:
                if t is None:
                    continue
                ch, v = t
                if v > 0 and v > hw.get(ch, 0):
                    eng.wait_ge(ch.sem, v)
                    hw[ch] = v

        def WF(eng, key, ch, v):
            hw = waited.setdefault(key, {})
            if v > 0 and v > hw.get(ch, 0):
                eng.wait_ge(ch.sem, v)
                hw[ch] = v

        c_u01 = sb("c_u01", [128, 128])
        c_a1 = sb("c_a1", [128, 128])
        c_a1b = sb("c_a1b", [128, 128], BF16)
        c_idf = sb("c_idf", [128, 128])
        c_idb = sb("c_idb", [128, 128], BF16)
        c_ones = sb("c_ones", [1, 128])
        c_onesb = sb("c_onesb", [1, 128], BF16)
        c_cbT = sb("c_cbT", [128, NXT])
        c_dtb = sb("c_dtb", [NH, 1])
        c_alog = sb("c_alog", [NH, 1])
        c_expa = sb("c_expa", [NH, 1])
        c_dbc = sb("c_dbc", [128, NH])
        c_nwT = sb("c_nwT", [128, 16])
        c_diag = sb("c_diag", [128, NXT * 4 * 128], BF16)
        c_wr = sb("c_wr", [128, 64], BF16)
        c_br = sb("c_br", [1, E], BF16)
        c_b2 = sb("c_b2", [E, D])
        c_eps = sb("c_eps", [128, 1])
        c_one = sb("c_one", [128, 1])
        c_b1 = sb("c_b1", [128, E * 32])

        with nc.Block() as blk:
            @blk.sync
            def _(y):
                for dst, src in [(c_u01, u01_h), (c_a1, a1s_h), (c_idf, idf_h),
                                 (c_ones, ones_h), (c_cbT, cbT_h), (c_dtb, dtb_h),
                                 (c_alog, alog_h), (c_dbc, dbc_h), (c_nwT, nwT_h),
                                 (c_b2, b2_h)]:
                    dSetS.inc(y.dma_start(dst[:, :], src[:, :]))
                dSetS.inc(y.dma_start(
                    bass.AP(c_b1, 0, [[E * 32, 128], [32, E], [1, 32]]),
                    bass.AP(b1T_h, 0, [[32, 128], [128 * 32, E], [1, 32]])))

            @blk.gpsimd
            def _(g):
                dSetG.inc(g.dma_start(c_a1b[:, :], a1s_h[:, :]))
                dSetG.inc(g.dma_start(c_idb[:, :], idf_h[:, :]))
                dSetG.inc(g.dma_start(c_onesb[:, :], ones_h[:, :]))
                dSetG.inc(g.dma_start(c_br[:, :], br_h[:, :]))
                dSetG.inc(g.dma_start(
                    bass.AP(c_wr, 0, [[64, 128], [8, 8], [1, 8]]),
                    bass.AP(wr_h, 0, [[8, 128], [128 * 8, 8], [1, 8]])))
                dSetG.inc(g.dma_start(
                    bass.AP(c_diag, 0, [[NXT * 4 * 128, 128], [128, NXT * 4], [1, 128]]),
                    bass.AP(cdiag_h, 0, [[128, 128], [128 * 128, NXT * 4], [1, 128]])))

            t_setup_x = dSetS.tok()
            t_setup_w = dSetG.tok()

            @blk.vector
            def _(v):
                v.memset(c_eps[:, :], 1e-5)
                cV.inc(v.memset(c_one[:, :], 1.0))

            @blk.scalar
            def _(s):
                W(s, "s", t_setup_x)
                cS.inc(s.activation(c_expa[:, :], c_alog[:, :], AF.Exp))

        m_ctx = ExitStack()
        with m_ctx:
            def msb(name, shape, dt=F32):
                return m_ctx.enter_context(nc.sbuf_tensor(name, shape, dt))

            m_wop = msb("m_wop", [128, 16 * D], BF16)
            m_xnt = msb("m_xnt", [128, 8 * SCL], BF16)
            m_xtm = msb("m_xtm", [128, NCH * D])
            m_xntm = msb("m_xntm", [128, NCH * D], BF16)
            m_z = msb("m_z", [128, NZT * SCL], BF16)
            m_xbc = msb("m_xbc", [128, NXT * XPW], BF16)
            m_dtraw = msb("m_dtraw", [NH, SCL])
            m_sp = msb("m_sp", [NH, SCL])
            m_t1 = msb("m_t1", [NH, SCL])
            m_dta = m_dtraw
            m_wip = msb("m_wip", [128, 4 * 1024], BF16)
            m_s = msb("m_s", [128, NH * HP])
            m_sbf = msb("m_sbf", [128, NH * HP], BF16)
            k_xs = msb("k_xs", [128, DIN], BF16)
            k_xd = msb("k_xd", [128, DIN], BF16)
            k_xdr = msb("k_xdr", [128, DIN], BF16)
            k_btm = msb("k_btm", [128, 128], BF16)
            k_g2u = msb("k_g2u", [128, 128], BF16)
            k_a2 = msb("k_a2", [128, 1024], BF16)
            k_m2 = msb("k_m2", [128, 2 * 1024], BF16)
            k_gm = msb("k_gm", [128, 2 * 1024], BF16)
            k_ytm = msb("k_ytm", [128, DIN])
            k_yrms = msb("k_yrms", [128, DIN], BF16)
            k_yrt = msb("k_yrt", [128, DIN], BF16)
            k_dtat = msb("k_dtat", [128, NH])
            k_dtt = msb("k_dtt", [128, NH])
            k_cumt = msb("k_cumt", [128, NH])
            k_bcpq = msb("k_bcpq", [128, NH])
            k_r = msb("k_r", [128, NH])
            k_pt = msb("k_pt", [128, NH])
            k_rarg = msb("k_rarg", [128, NH])
            k_cumq = msb("k_cumq", [NH, 1])
            k_cumqr = msb("k_cumqr", [1, NH])
            k_pqr = msb("k_pqr", [1, NH])
            k_ln = msb("k_ln", [128, 24])
            k_ssq = msb("k_ssq", [128, 1])
            k_rstd = msb("k_rstd", [128, 1])
            b_base = msb("b_base", [128, 2 * D])

            p0 = m_ctx.enter_context(nc.psum_tensor("p0", [128, 512], F32))
            p1 = m_ctx.enter_context(nc.psum_tensor("p1", [128, 512], F32))
            pT1 = m_ctx.enter_context(nc.psum_tensor("pT1", [128, 1024], BF16))
            pT2 = m_ctx.enter_context(nc.psum_tensor("pT2", [128, 1024], BF16))
            pE = m_ctx.enter_context(nc.psum_tensor("pE", [128, 512], F32))
            pF = m_ctx.enter_context(nc.psum_tensor("pF", [128, 512], F32))
            pG = m_ctx.enter_context(nc.psum_tensor("pG", [128, 512], F32))
            pM = m_ctx.enter_context(nc.psum_tensor("pM", [128, 512], F32))

            with nc.Block() as blk:
                @blk.gpsimd
                def _(g):
                    for kt in range(16):
                        dWop.inc(g.dma_start(
                            m_wop[:, kt * D:(kt + 1) * D],
                            bass.AP(wopT_h, kt * 128 * D, [[D, 128], [1, D]])))
                t_wop_dma = dWop.tok()

                @blk.vector
                def _(v):
                    W(v, "v", t_wop_dma)
                    for kt in range(16):
                        ins = v.tensor_scalar(m_wop[:, kt * D:(kt + 1) * D],
                                              m_wop[:, kt * D:(kt + 1) * D],
                                              c_nwT[:, kt:kt + 1], None, ALU.mult)
                    cV.inc(ins)
                t_wop = cV.tok()

                @blk.vector
                def _(v):
                    v.memset(m_s[:, :], 0.0)
                    v.memset(m_sbf[:, :], 0.0)
                    ins = v.memset(m_xbc[:, :], 0.0)
                    cV.inc(ins)
                t_minit = cV.tok()

                st = {
                    "xtm_free": None, "xntm_free": None, "xnt_free": None,
                    "z_free": None, "xbc_free": [None] * NXT,
                    "pT_free": [None, None], "pM_free": None,
                    "p01_free": [t_minit, t_minit],
                    "pE_free": t_minit, "pF_free": t_minit, "pG_free": t_minit,
                    "wip_slot": [None] * 4,
                    "a2_free": None, "m2_free": [None, None],
                    "gm_free": [None, None],
                    "state_w": t_minit, "base_free": [None, None],
                    "dt_free": None,
                }
                wip_f_global = [0]

                def emit_sc(sc):
                    @blk.sync
                    def _(y):
                        W(y, "y", st["xtm_free"])
                        t = dXld.inc(y.dma_start(
                            m_xtm[:, :],
                            bass.AP(x_h, sc * SCL * D, [[D, 128], [128 * D, NCH], [1, D]])))
                        st["x_ld"] = t

                    @blk.vector
                    def _(v):
                        W(v, "v", st["x_ld"], st["xntm_free"], st.get("yrms_rd"))
                        for t4 in range(NCH):
                            xs_ = m_xtm[:, t4 * D:(t4 + 1) * D]
                            col = lambda j: k_ln[:, t4 * 6 + j:t4 * 6 + j + 1]
                            v.reduce_sum(col(0), xs_, axis=AXX)
                            v.scalar_tensor_tensor(k_yrms[:, 0:1024], xs_, 1.0, xs_,
                                                   ALU.mult, ALU.mult, accum_out=col(2))
                            v.drain()
                            v.tensor_scalar(col(1), col(0), -1.0 / D, None, ALU.mult)
                            v.drain()
                            v.tensor_tensor(col(3), col(1), col(1), ALU.mult)
                            v.drain()
                            ins = v.scalar_tensor_tensor(col(4), col(2), 1.0 / D, col(3),
                                                         ALU.mult, ALU.subtract)
                            cV.inc(ins)
                    t_var = cV.tok()

                    @blk.scalar
                    def _(s):
                        W(s, "s", t_var)
                        for t4 in range(NCH):
                            col = lambda j: k_ln[:, t4 * 6 + j:t4 * 6 + j + 1]
                            s.activation(col(5), col(4), AF.Ln, bias=c_eps[:, :])
                            s.drain()
                            ins = s.activation(col(5), col(5), AF.Exp, scale=-0.5)
                        cS.inc(ins)
                    t_rstd = cS.tok()

                    @blk.vector
                    def _(v):
                        W(v, "v", t_rstd)
                        for t4 in range(NCH):
                            xs_ = m_xtm[:, t4 * D:(t4 + 1) * D]
                            col = lambda j: k_ln[:, t4 * 6 + j:t4 * 6 + j + 1]
                            ins = v.tensor_scalar(m_xntm[:, t4 * D:(t4 + 1) * D],
                                                  xs_, col(1), col(5), ALU.add, ALU.mult)
                            cV.inc(ins)
                            st[("ln", t4)] = cV.tok()

                    for t4 in range(NCH):
                        pt = pT1 if t4 % 2 == 0 else pT2

                        @blk.tensor
                        def _(t, t4=t4, pt=pt):
                            W(t, "t", st[("ln", t4)], st["pT_free"][t4 % 2], st["xnt_free"])
                            for kd in range(8):
                                ins = t.transpose(
                                    pt[:, kd * 128:(kd + 1) * 128],
                                    m_xntm[:, t4 * D + kd * 128:t4 * D + (kd + 1) * 128],
                                    c_idb[:, :])
                            cT.inc(ins)
                        t_tp = cT.tok()

                        @blk.vector
                        def _(v, t4=t4, pt=pt, t_tp=t_tp):
                            W(v, "v", t_tp)
                            ins = v.tensor_copy(
                                bass.AP(m_xnt, t4 * 128, [[8 * SCL, 128], [SCL, 8], [1, 128]]),
                                bass.AP(pt, 0, [[1024, 128], [128, 8], [1, 128]]))
                            cV.inc(ins)
                        st["pT_free"][t4 % 2] = cV.tok()
                        st[("xnt", t4)] = cV.tok()

                    t_xnt_all = cV.tok()

                    @blk.scalar
                    def _(s):
                        W(s, "s", t_xnt_all)
                        t = dSpill.inc(s.dma_start(
                            bass.AP(xnt_hbm, sc * 128 * L, [[L, 128], [1, 8 * SCL]]),
                            m_xnt[:, :]))
                        st["spill"] = t

                    if "zx" in debug:
                        @blk.sync
                        def _(y):
                            W(y, "y", t_xnt_all)
                            dX.inc(y.dma_start(
                                bass.AP(dbg["xn"], sc * 128 * (NCH * D), [[NCH * D, 128], [1, NCH * D]]),
                                m_xntm[:, :]))

                    cT0_ip = cT.v
                    cS0_ip = cS.v

                    @blk.gpsimd
                    def _(g):
                        for f in range(35):
                            slot = f % 4
                            wd = 128 if f < 34 else 32
                            tok = st["wip_slot"][slot]
                            if f >= 4:
                                WF(g, "g", cT, cT0_ip + f - 4 + 1)
                            elif tok is not None:
                                W(g, "g", tok)
                            ins = g.dma_start(
                                bass.AP(m_wip, slot * 1024, [[4 * 1024, 128], [wd, 8], [1, wd]]),
                                bass.AP(wipT_h, f * 128, [[DIP, 128], [128 * DIP, 8], [1, wd]]))
                            st[("wipd", f)] = rWip[slot].inc(ins)
                            st["wip_slot"][slot] = (cT, cT0_ip + f + 1)
                            wip_f_global[0] += 1

                    @blk.tensor
                    def _(t):
                        W(t, "t", t_xnt_all, t_wop, st["p01_free"][0], st["p01_free"][1])
                        for f in range(35):
                            slot = f % 4
                            wd = 128 if f < 34 else 32
                            pp = p0 if f % 2 == 0 else p1
                            W(t, "t", st[("wipd", f)])
                            if f >= 2:
                                WF(t, "t", cS, cS0_ip + f - 2 + 1)
                            for kd in range(8):
                                ins = t.matmul(pp[:wd, :],
                                               m_wip[:, slot * 1024 + kd * wd:slot * 1024 + (kd + 1) * wd],
                                               m_xnt[:, kd * SCL:(kd + 1) * SCL],
                                               start=(kd == 0), stop=(kd == 7))
                            cT.inc(ins)

                    @blk.scalar
                    def _(s):
                        for f in range(35):
                            pp = p0 if f % 2 == 0 else p1
                            WF(s, "s", cT, cT0_ip + f + 1)
                            if f < NZT:
                                ins = s.activation(m_z[:, f * SCL:(f + 1) * SCL], pp[:, :], AF.Silu)
                            elif f < 34:
                                ct = f - NZT
                                ins = s.activation(
                                    bass.AP(m_xbc, ct * XPW + 4, [[NXT * XPW, 128], [1, SCL]]),
                                    pp[:, :], AF.Copy)
                            else:
                                ins = s.activation(m_dtraw[:, :], pp[:NH, :], AF.Copy)
                            cS.inc(ins)
                            st[("ipev", f)] = cS.tok()
                    st["p01_free"][0] = (cS, cS0_ip + 35)
                    st["p01_free"][1] = (cS, cS0_ip + 35)

                    @blk.scalar
                    def _(s):
                        W(s, "s", st[("ipev", 34)])
                        s.activation(m_t1[:, :], m_dtraw[:, :], AF.Abs, bias=c_dtb[:, :])
                        s.drain()
                        s.activation(m_t1[:, :], m_t1[:, :], AF.Exp, scale=-1.0)
                        s.drain()
                        s.activation(m_t1[:, :], m_t1[:, :], AF.Ln, bias=c_one[0:NH, :])
                        s.drain()
                        ins = s.activation(m_sp[:, :], m_dtraw[:, :], AF.Relu, bias=c_dtb[:, :])
                        cS.inc(ins)
                    t_sp1 = cS.tok()

                    @blk.vector
                    def _(v):
                        W(v, "v", t_sp1, t_setup_x)
                        v.tensor_tensor(m_sp[:, :], m_sp[:, :], m_t1[:, :], ALU.add)
                        ins = v.tensor_scalar(m_dta[:, :], m_sp[:, :], c_expa[:, :], -1.0,
                                              ALU.mult, ALU.mult)
                        cV.inc(ins)
                    st["dtA"] = cV.tok()
                    st["dt_free"] = None

                    cT0_cv = cT.v
                    cS0_cv = cS.v

                    @blk.tensor
                    def _(t):
                        W(t, "t", t_setup_w, st["p01_free"][0], st["p01_free"][1])
                        for ct in range(NXT):
                            pp = p0 if ct % 2 == 0 else p1
                            W(t, "t", st[("ipev", 16 + ct)])
                            if ct >= 2:
                                WF(t, "t", cS, cS0_cv + ct - 2 + 1)
                            for k in range(4):
                                ins = t.matmul(
                                    pp[:, :],
                                    c_diag[:, (ct * 4 + k) * 128:(ct * 4 + k + 1) * 128],
                                    bass.AP(m_xbc, ct * XPW + 1 + k, [[NXT * XPW, 128], [1, SCL]]),
                                    start=(k == 0), stop=(k == 3))
                            cT.inc(ins)

                    @blk.vector
                    def _(v):
                        for ct in range(NXT):
                            WF(v, "v", cT, cT0_cv + ct + 1)
                            ins = v.tensor_copy(
                                bass.AP(m_xbc, ct * XPW + 1, [[NXT * XPW, 128], [1, 3]]),
                                bass.AP(m_xbc, ct * XPW + 513, [[NXT * XPW, 128], [1, 3]]))
                            cV.inc(ins)
                    cV0_cv = cV.v

                    @blk.scalar
                    def _(s):
                        for ct in range(NXT):
                            pp = p0 if ct % 2 == 0 else p1
                            WF(s, "s", cV, cV0_cv - NXT + ct + 1)
                            ins = s.activation(
                                bass.AP(m_xbc, ct * XPW + 4, [[NXT * XPW, 128], [1, SCL]]),
                                pp[:, :], AF.Silu, bias=c_cbT[:, ct:ct + 1])
                            cS.inc(ins)
                            st[("cv", ct)] = cS.tok()
                    st["p01_free"][0] = (cS, cS0_cv + NXT)
                    st["p01_free"][1] = (cS, cS0_cv + NXT)

                    t_conv_all = cS.tok()

                    if "zx" in debug:
                        @blk.sync
                        def _(y):
                            W(y, "y", st[("ipev", 15)], t_conv_all, st["dtA"])
                            dX.inc(y.dma_start(
                                bass.AP(dbg["z"], sc * 128 * (NZT * SCL), [[NZT * SCL, 128], [1, NZT * SCL]]),
                                m_z[:, :]))
                            dX.inc(y.dma_start(
                                bass.AP(dbg["xbc"], sc * 128 * (NXT * XPW), [[NXT * XPW, 128], [1, NXT * XPW]]),
                                m_xbc[:, :]))
                            dX.inc(y.dma_start(
                                bass.AP(dbg["dt"], sc * NH * SCL, [[SCL, NH], [1, SCL]]),
                                m_dtraw[:, :]))
                            dX.inc(y.dma_start(
                                bass.AP(dbg["sp"], sc * NH * SCL, [[SCL, NH], [1, SCL]]),
                                m_sp[:, :]))

                    for c in range(NCH):
                        emit_chunk(sc, c)

                    st["xtm_free"] = cV.tok()
                    st["xntm_free"] = cV.tok()
                    st["xnt_free"] = st["spill"]
                    st["z_free"] = cV.tok()
                    for ct in range(NXT):
                        st["xbc_free"][ct] = cV.tok()
                    st["dt_free"] = cV.tok()

                def emit_chunk(sc, c):
                    co = c * Q
                    @blk.vector
                    def _(v):
                        W(v, "v", st["dtA"])
                        ins = v.reduce_sum(k_cumq[:, :], m_dta[:, co:co + Q], axis=AXX)
                        cV.inc(ins)
                    t_cumq = cV.tok()

                    @blk.tensor
                    def _(t):
                        W(t, "t", t_cumq, st["pM_free"])
                        t.transpose(pM[:, 0:32], m_dta[:NH, co:co + Q], c_idf[0:NH, 0:NH])
                        t.transpose(pM[:, 32:64], m_sp[:NH, co:co + Q], c_idf[0:NH, 0:NH])
                        ins = t.transpose(pM[0:1, 64:96], k_cumq[:, :], c_idf[0:NH, 0:NH])
                        cT.inc(ins)
                    t_rA = cT.tok()

                    @blk.vector
                    def _(v):
                        W(v, "v", t_rA)
                        v.tensor_copy(k_dtat[:, :], pM[:, 0:32])
                        v.tensor_copy(k_dtt[:, :], pM[:, 32:64])
                        ins = v.tensor_copy(k_cumqr[:, :], pM[0:1, 64:96])
                        cV.inc(ins)
                    t_rAev = cV.tok()

                    @blk.scalar
                    def _(s):
                        W(s, "s", t_rAev)
                        ins = s.activation(k_pqr[:, :], k_cumqr[:, :], AF.Exp)
                        cS.inc(ins)
                    t_pqr = cS.tok()

                    @blk.tensor
                    def _(t):
                        W(t, "t", t_rAev, t_pqr)
                        t.matmul(pM[:, 0:32], c_u01[:, :], k_dtat[:, :], start=True, stop=True)
                        t.matmul(pM[:, 32:64], c_ones[:, :], k_cumqr[:, :], start=True, stop=True)
                        ins = t.matmul(pM[:, 64:96], c_ones[:, :], k_pqr[:, :], start=True, stop=True)
                        cT.inc(ins)
                    t_rB = cT.tok()

                    @blk.vector
                    def _(v):
                        W(v, "v", t_rB)
                        v.tensor_copy(k_cumt[:, :], pM[:, 0:32])
                        v.drain()
                        v.tensor_tensor(k_rarg[:, :], pM[:, 32:64], k_cumt[:, :], ALU.subtract)
                        ins = v.tensor_copy(k_bcpq[:, :], pM[:, 64:96])
                        cV.inc(ins)
                    st["pM_free"] = cV.tok()
                    t_rBev = cV.tok()

                    @blk.scalar
                    def _(s):
                        W(s, "s", t_rBev)
                        s.activation(k_pt[:, :], k_cumt[:, :], AF.Exp)
                        ins = s.activation(k_r[:, :], k_rarg[:, :], AF.Exp)
                        cS.inc(ins)
                    t_ptr = cS.tok()

                    if "ssd" in debug:
                        @blk.sync
                        def _(y):
                            W(y, "y", t_rBev)
                            dX.inc(y.dma_start(
                                bass.AP(dbg["cumt"], (sc * NCH + c) * 128 * NH, [[NH, 128], [1, NH]]),
                                k_cumt[:, :]))

                    @blk.tensor
                    def _(t):
                        W(t, "t", st[("cv", 16)], st[("cv", 0)], st[("cv", 6)],
                          st["pT_free"][0], st["pT_free"][1])
                        for j in range(7):
                            W(t, "t", st[("cv", j)])
                        t.transpose(pT1[:, 0:128],
                                    bass.AP(m_xbc, 16 * XPW + 4 + co, [[NXT * XPW, 128], [1, Q]]),
                                    c_idb[:, :])
                        for j in range(7):
                            ins = t.transpose(pT1[:, (j + 1) * 128:(j + 2) * 128],
                                              bass.AP(m_xbc, j * XPW + 4 + co, [[NXT * XPW, 128], [1, Q]]),
                                              c_idb[:, :])
                        cT.inc(ins)
                    t_tpa = cT.tok()

                    @blk.tensor
                    def _(t):
                        for j in range(7, 15):
                            W(t, "t", st[("cv", j)])
                        for j in range(7, 15):
                            ins = t.transpose(pT2[:, (j - 7) * 128:(j - 6) * 128],
                                              bass.AP(m_xbc, j * XPW + 4 + co, [[NXT * XPW, 128], [1, Q]]),
                                              c_idb[:, :])
                        cT.inc(ins)
                    t_tpb = cT.tok()

                    @blk.vector
                    def _(v):
                        W(v, "v", t_tpa, t_tpb)
                        v.tensor_copy(k_btm[:, :], pT1[:, 0:128])
                        v.tensor_copy(k_xs[:, 0:7 * 128], pT1[:, 128:1024])
                        ins = v.tensor_copy(k_xs[:, 7 * 128:15 * 128], pT2[:, :])
                        cV.inc(ins)
                    t_ev1 = cV.tok()
                    st["pT_free"][0] = t_ev1
                    st["pT_free"][1] = t_ev1

                    @blk.tensor
                    def _(t):
                        W(t, "t", st[("cv", 15)], t_ev1)
                        ins = t.transpose(pT1[:, 0:128],
                                          bass.AP(m_xbc, 15 * XPW + 4 + co, [[NXT * XPW, 128], [1, Q]]),
                                          c_idb[:, :])
                        cT.inc(ins)
                    t_tpc = cT.tok()

                    @blk.vector
                    def _(v):
                        W(v, "v", t_tpc)
                        ins = v.tensor_copy(k_xs[:, 15 * 128:DIN], pT1[:, 0:128])
                        cV.inc(ins)
                    st["pT_free"][0] = cV.tok()
                    t_xs = cV.tok()

                    @blk.vector
                    def _(v):
                        W(v, "v", t_xs, t_ptr)
                        for h in range(NH):
                            v.tensor_scalar(k_xd[:, h * HP:(h + 1) * HP],
                                            k_xs[:, h * HP:(h + 1) * HP],
                                            k_dtt[:, h:h + 1], None, ALU.mult)
                        for h in range(NH):
                            ins = v.tensor_scalar(k_xdr[:, h * HP:(h + 1) * HP],
                                                  k_xd[:, h * HP:(h + 1) * HP],
                                                  k_r[:, h:h + 1], None, ALU.mult)
                        cV.inc(ins)
                    t_xdr = cV.tok()

                    @blk.tensor
                    def _(t):
                        W(t, "t", st[("cv", 16)], st[("cv", 17)], st["pE_free"])
                        ins = t.matmul(pE[:, 0:128],
                                       bass.AP(m_xbc, 16 * XPW + 4 + co, [[NXT * XPW, 128], [1, Q]]),
                                       bass.AP(m_xbc, 17 * XPW + 4 + co, [[NXT * XPW, 128], [1, Q]]),
                                       start=True, stop=True)
                        cT.inc(ins)
                    t_g2 = cT.tok()

                    @blk.vector
                    def _(v):
                        W(v, "v", t_g2)
                        ins = v.tensor_tensor(k_g2u[:, :], pE[:, 0:128], c_u01[:, :], ALU.mult)
                        cV.inc(ins)
                    t_g2u = cV.tok()
                    st["pE_free"] = t_g2u

                    for g in range(4):
                        gb = g % 2
                        hh0 = g * 8

                        @blk.vector
                        def _(v, g=g, gb=gb, hh0=hh0):
                            W(v, "v", t_rAev, st["a2_free"])
                            for i in range(8):
                                ins = v.tensor_scalar(k_a2[:, i * 128:(i + 1) * 128],
                                                      c_u01[:, :], k_dtat[:, hh0 + i:hh0 + i + 1],
                                                      None, ALU.mult)
                            cV.inc(ins)
                        t_a2 = cV.tok()

                        @blk.tensor
                        def _(t, g=g, gb=gb, t_a2=t_a2):
                            W(t, "t", t_a2, st["p01_free"][0], st["p01_free"][1])
                            for i in range(8):
                                pp = p0 if i < 4 else p1
                                ins = t.matmul(pp[:, (i % 4) * 128:(i % 4 + 1) * 128],
                                               c_a1b[:, :],
                                               k_a2[:, i * 128:(i + 1) * 128],
                                               start=True, stop=True)
                            cT.inc(ins)
                        t_d2 = cT.tok()
                        st["a2_free"] = t_d2

                        @blk.scalar
                        def _(s, gb=gb, t_d2=t_d2):
                            W(s, "s", t_d2, st["m2_free"][gb])
                            s.activation(k_m2[:, gb * 1024:gb * 1024 + 512], p0[:, :], AF.Exp)
                            ins = s.activation(k_m2[:, gb * 1024 + 512:gb * 1024 + 1024], p1[:, :], AF.Exp)
                            cS.inc(ins)
                        t_m2 = cS.tok()
                        st["p01_free"][0] = t_m2
                        st["p01_free"][1] = t_m2

                        @blk.vector
                        def _(v, gb=gb, t_m2=t_m2):
                            W(v, "v", t_m2, t_g2u, st["gm_free"][gb])
                            for i in range(8):
                                ins = v.tensor_tensor(
                                    k_gm[:, gb * 1024 + i * 128:gb * 1024 + (i + 1) * 128],
                                    k_g2u[:, :],
                                    k_m2[:, gb * 1024 + i * 128:gb * 1024 + (i + 1) * 128],
                                    ALU.mult)
                            cV.inc(ins)
                        t_gm = cV.tok()
                        st["m2_free"][gb] = t_gm

                        @blk.tensor
                        def _(t, g=g, gb=gb, hh0=hh0, t_gm=t_gm):
                            W(t, "t", t_gm, t_xdr, st["state_w"], st.get(("sbf_prev", g)),
                              st["pE_free"], st["pF_free"], st["pG_free"])
                            for i in range(8):
                                h = hh0 + i
                                t.matmul(pE[:, i * HP:(i + 1) * HP],
                                         k_gm[:, gb * 1024 + i * 128:gb * 1024 + (i + 1) * 128],
                                         k_xd[:, h * HP:(h + 1) * HP], start=True, stop=True)
                            for i in range(8):
                                h = hh0 + i
                                t.matmul(pF[:, i * HP:(i + 1) * HP],
                                         bass.AP(m_xbc, 17 * XPW + 4 + co, [[NXT * XPW, 128], [1, Q]]),
                                         m_sbf[:, h * HP:(h + 1) * HP], start=True, stop=True)
                            for i in range(8):
                                h = hh0 + i
                                ins = t.matmul(pG[:, i * HP:(i + 1) * HP],
                                               k_btm[:, :],
                                               k_xdr[:, h * HP:(h + 1) * HP], start=True, stop=True)
                            cT.inc(ins)
                        t_y = cT.tok()
                        st["gm_free"][gb] = t_y

                        @blk.vector
                        def _(v, g=g, hh0=hh0, t_y=t_y):
                            W(v, "v", t_y, t_ptr, st.get(("sbf_prev", g)))
                            v.tensor_copy(k_ytm[:, g * 512:(g + 1) * 512], pE[:, :])
                            for i in range(8):
                                h = hh0 + i
                                sl = k_ytm[:, h * HP:(h + 1) * HP]
                                v.scalar_tensor_tensor(sl, pF[:, i * HP:(i + 1) * HP],
                                                       k_pt[:, h:h + 1], sl, ALU.mult, ALU.add)
                            for i in range(8):
                                h = hh0 + i
                                sl = k_ytm[:, h * HP:(h + 1) * HP]
                                v.scalar_tensor_tensor(sl, k_xs[:, h * HP:(h + 1) * HP],
                                                       c_dbc[:, h:h + 1], sl, ALU.mult, ALU.add)
                            for i in range(8):
                                h = hh0 + i
                                sl = m_s[:, h * HP:(h + 1) * HP]
                                v.scalar_tensor_tensor(sl, sl, k_bcpq[:, h:h + 1],
                                                       pG[:, i * HP:(i + 1) * HP],
                                                       ALU.mult, ALU.add)
                            for i in range(8):
                                h = hh0 + i
                                ins = v.tensor_copy(m_sbf[:, h * HP:(h + 1) * HP],
                                                    m_s[:, h * HP:(h + 1) * HP])
                            cV.inc(ins)
                        t_yev = cV.tok()
                        st[("sbf", g)] = t_yev
                        st["pE_free"] = t_yev
                        st["pF_free"] = t_yev
                        st["pG_free"] = t_yev
                    st["state_w"] = cV.tok()
                    for g in range(4):
                        st[("sbf_prev", g)] = st.get(("sbf", g))

                    @blk.tensor
                    def _(t):
                        W(t, "t", st[("ipev", 15)], st["pT_free"][0], st["pT_free"][1],
                          st["state_w"])
                        for j in range(8):
                            t.transpose(pT1[:, j * 128:(j + 1) * 128],
                                        m_z[:, j * SCL + co:j * SCL + co + Q], c_idb[:, :])
                        for j in range(8, 16):
                            ins = t.transpose(pT2[:, (j - 8) * 128:(j - 7) * 128],
                                              m_z[:, j * SCL + co:j * SCL + co + Q], c_idb[:, :])
                        cT.inc(ins)
                    t_ztp = cT.tok()

                    @blk.vector
                    def _(v):
                        W(v, "v", t_ztp)
                        v.tensor_tensor(k_ytm[:, 0:1024], k_ytm[:, 0:1024], pT1[:, :], ALU.mult)
                        ins = v.tensor_tensor(k_ytm[:, 1024:2048], k_ytm[:, 1024:2048], pT2[:, :], ALU.mult)
                        cV.inc(ins)
                    t_gate = cV.tok()
                    st["pT_free"][0] = t_gate
                    st["pT_free"][1] = t_gate

                    if "ssd" in debug:
                        @blk.sync
                        def _(y):
                            W(y, "y", t_gate)
                            dX.inc(y.dma_start(
                                bass.AP(dbg["ytm"], (sc * NCH + c) * 128 * DIN, [[DIN, 128], [1, DIN]]),
                                k_ytm[:, :]))

                    @blk.vector
                    def _(v):
                        W(v, "v", t_gate)
                        ins = v.scalar_tensor_tensor(k_yrms[:, :], k_ytm[:, :], 1.0, k_ytm[:, :],
                                                     ALU.mult, ALU.mult, accum_out=k_ssq[:, :])
                        cV.inc(ins)
                    t_ssq = cV.tok()

                    @blk.scalar
                    def _(s):
                        W(s, "s", t_ssq)
                        s.activation(k_rstd[:, :], k_ssq[:, :], AF.Ln, bias=c_eps[:, :], scale=1.0 / DIN)
                        s.drain()
                        ins = s.activation(k_rstd[:, :], k_rstd[:, :], AF.Exp, scale=-0.5)
                        cS.inc(ins)
                    t_rstd = cS.tok()

                    @blk.vector
                    def _(v):
                        W(v, "v", t_rstd)
                        ins = v.tensor_scalar(k_yrms[:, :], k_ytm[:, :], k_rstd[:, :], None, ALU.mult)
                        cV.inc(ins)
                    t_yrms = cV.tok()

                    if "ssd" in debug:
                        @blk.sync
                        def _(y):
                            W(y, "y", t_yrms)
                            dX.inc(y.dma_start(
                                bass.AP(dbg["yrms"], (sc * NCH + c) * 128 * DIN, [[DIN, 128], [1, DIN]]),
                                k_yrms[:, :]))

                    @blk.tensor
                    def _(t):
                        W(t, "t", t_yrms, st["pT_free"][0], st["pT_free"][1])
                        for kt in range(8):
                            t.transpose(pT1[:, kt * 128:(kt + 1) * 128],
                                        k_yrms[:, kt * 128:(kt + 1) * 128], c_idb[:, :])
                        for kt in range(8, 16):
                            ins = t.transpose(pT2[:, (kt - 8) * 128:(kt - 7) * 128],
                                              k_yrms[:, kt * 128:(kt + 1) * 128], c_idb[:, :])
                        cT.inc(ins)
                    t_ytp = cT.tok()
                    st["yrms_rd"] = t_ytp

                    @blk.vector
                    def _(v):
                        W(v, "v", t_ytp)
                        v.tensor_copy(k_yrt[:, 0:1024], pT1[:, :])
                        ins = v.tensor_copy(k_yrt[:, 1024:2048], pT2[:, :])
                        cV.inc(ins)
                    t_yrt = cV.tok()
                    st["pT_free"][0] = t_yrt
                    st["pT_free"][1] = t_yrt

                    @blk.tensor
                    def _(t):
                        W(t, "t", t_yrt, st["pE_free"], st["pF_free"], t_wop)
                        for kt in range(16):
                            t.matmul(pE[:, :], k_yrt[:, kt * 128:(kt + 1) * 128],
                                     m_wop[:, kt * D:kt * D + 512],
                                     start=(kt == 0), stop=(kt == 15))
                        for kt in range(16):
                            ins = t.matmul(pF[:, :], k_yrt[:, kt * 128:(kt + 1) * 128],
                                           m_wop[:, kt * D + 512:(kt + 1) * D],
                                           start=(kt == 0), stop=(kt == 15))
                        cT.inc(ins)
                    t_op = cT.tok()

                    bslot = c % 2

                    @blk.vector
                    def _(v, c=c, bslot=bslot, t_op=t_op):
                        W(v, "v", t_op, st["base_free"][bslot])
                        bb = b_base[:, bslot * D:(bslot + 1) * D]
                        v.tensor_tensor(bb, m_xtm[:, c * D:(c + 1) * D],
                                        m_xntm[:, c * D:(c + 1) * D], ALU.add)
                        v.tensor_tensor(b_base[:, bslot * D:bslot * D + 512],
                                        b_base[:, bslot * D:bslot * D + 512], pE[:, :], ALU.add)
                        ins = v.tensor_tensor(b_base[:, bslot * D + 512:(bslot + 1) * D],
                                              b_base[:, bslot * D + 512:(bslot + 1) * D], pF[:, :], ALU.add)
                        cV.inc(ins)
                    t_base = cV.tok()
                    st["pE_free"] = t_base
                    st["pF_free"] = t_base

                    @blk.sync
                    def _(y, sc=sc, c=c, bslot=bslot, t_base=t_base):
                        W(y, "y", t_base)
                        ins = y.dma_start(
                            bass.AP(out_h, (sc * SCL + c * Q) * D, [[D, 128], [1, D]]),
                            b_base[:, bslot * D:(bslot + 1) * D])
                        st["base_free"][bslot] = dB[bslot].inc(ins)

                for sc in range(n_sc):
                    emit_sc(sc)

                t_base_all = None
                t_base_all2 = [dB[0].tok(), dB[1].tok()]
                t_spill_all = dSpill.tok()

                @blk.sync
                def _(y):
                    W(y, "y", *t_base_all2)

        if do_ffn:
            f_ctx = ExitStack()
            with f_ctx:
                def fsb(name, shape, dt=F32):
                    return f_ctx.enter_context(nc.sbuf_tensor(name, shape, dt))

                f_xnt = fsb("f_xnt", [128, 2 * L], BF16)
                f_ht = fsb("f_ht", [128, 64 * 512], BF16)
                f_acc = fsb("f_acc", [128, 2 * 8 * D])
                f_w1 = fsb("f_w1", [128, 4 * 1024], BF16)
                f_w2 = fsb("f_w2", [128, 8 * 512], BF16)
                f_lg = fsb("f_lg", [128, 64])
                f_w = fsb("f_w", [128, 256])
                f_wt = fsb("f_wt", [8, L // 128 * 128])
                f_sm = fsb("f_sm", [128, 64])
                f_s8a = fsb("f_s8a", [128, 64])
                f_s8b = fsb("f_s8b", [128, 64])

                q = [f_ctx.enter_context(nc.psum_tensor(f"q{i}", [128, 512], F32))
                     for i in range(8)]

                with nc.Block() as blk:
                    fst = {
                        "q_free": [None] * 8,
                        "w1_slot": [None] * 4,
                        "w2_slot": [None] * 8,
                        "acc_free": [None, None],
                        "xnt_free": None,
                        "ht_tok": None,
                    }

                    def emit_pair(p):
                        aslot = p % 2
                        @blk.scalar
                        def _(s, p=p):
                            W(s, "s", t_spill_all, fst["xnt_free"])
                            for sub01 in range(2):
                                sub = p * 2 + sub01
                                ins = s.dma_start(
                                    f_xnt[:, sub01 * L:sub01 * L + L],
                                    bass.AP(xnt_hbm, sub * 128 * L, [[L, 128], [1, L]]))
                                fst[("xntd", sub01)] = dXnt.inc(ins)

                        @blk.sync
                        def _(y, p=p, aslot=aslot):
                            W(y, "y", *t_base_all2)
                            W(y, "y", fst["acc_free"][aslot])
                            ins = y.dma_start(
                                f_acc[:, aslot * 8 * D:(aslot + 1) * 8 * D],
                                bass.AP(out_h, p * 1024 * D, [[D, 128], [128 * D, 8], [1, D]]))
                            fst["base_ld"] = dAL.inc(ins)

                        @blk.tensor
                        def _(t, p=p):
                            W(t, "t", fst[("xntd", 0)], fst[("xntd", 1)], t_setup_w, t_setup_x)
                            W(t, "t", fst["q_free"][7])
                            for i in range(8):
                                sub01, t128 = i // 4, i % 4
                                t.matmul(q[7][:, i * 8:(i + 1) * 8], c_onesb[:, :], c_br[:, :],
                                         start=True, stop=False)
                                for kd in range(8):
                                    ins = t.matmul(
                                        q[7][:, i * 8:(i + 1) * 8],
                                        f_xnt[:, sub01 * L + kd * 512 + t128 * 128:
                                              sub01 * L + kd * 512 + (t128 + 1) * 128],
                                        c_wr[:, kd * 8:(kd + 1) * 8],
                                        start=False, stop=(kd == 7))
                            cT.inc(ins)
                        t_lg = cT.tok()

                        @blk.vector
                        def _(v, t_lg=t_lg):
                            W(v, "v", t_lg)
                            ins = v.tensor_copy(f_lg[:, :], q[7][:, 0:64])
                            cV.inc(ins)
                        t_lgev = cV.tok()
                        fst["q_free"][7] = t_lgev

                        sl = lambda i, j: f_sm[:, i * 8 + j:i * 8 + j + 1]
                        lgs = lambda i: f_lg[:, i * 8:(i + 1) * 8]
                        s8a = lambda i: f_s8a[:, i * 8:(i + 1) * 8]
                        s8b = lambda i: f_s8b[:, i * 8:(i + 1) * 8]

                        @blk.vector
                        def _(v):
                            W(v, "v", t_lgev)
                            for i in range(8):
                                v.reduce_max(sl(i, 0), lgs(i), axis=AXX)
                            for i in range(8):
                                v.tensor_scalar(sl(i, 1), sl(i, 0), -1.0, None, ALU.mult)
                            for i in range(8):
                                v.tensor_scalar(s8a(i), lgs(i), sl(i, 0), None, ALU.is_ge)
                            for i in range(8):
                                v.scalar_tensor_tensor(s8b(i), s8a(i), -1e30, lgs(i),
                                                       ALU.mult, ALU.add)
                            for i in range(8):
                                ins = v.reduce_max(sl(i, 2), s8b(i), axis=AXX)
                            cV.inc(ins)
                        t_m2x = cV.tok()

                        @blk.scalar
                        def _(s):
                            W(s, "s", t_m2x)
                            for i in range(8):
                                ins = s.activation(s8a(i), lgs(i), AF.Exp, bias=sl(i, 1))
                            cS.inc(ins)
                        t_expv = cS.tok()

                        @blk.vector
                        def _(v, p=p):
                            W(v, "v", t_expv)
                            for i in range(8):
                                v.tensor_scalar(s8b(i), lgs(i), sl(i, 2), None, ALU.is_ge)
                            for i in range(8):
                                v.tensor_tensor(s8a(i), s8a(i), s8b(i), ALU.mult)
                            for i in range(8):
                                v.reduce_sum(sl(i, 3), s8a(i), axis=AXX)
                            for i in range(8):
                                v.reciprocal(sl(i, 4), sl(i, 3))
                            for i in range(8):
                                ins = v.tensor_scalar(f_w[:, (p * 8 + i) * 8:(p * 8 + i + 1) * 8],
                                                      s8a(i), sl(i, 4), None, ALU.mult)
                            cV.inc(ins)
                        t_wall = cV.tok()
                        for i in range(8):
                            fst[("wrt", i)] = t_wall

                        @blk.tensor
                        def _(t, p=p):
                            W(t, "t", fst["q_free"][6], fst["q_free"][5])
                            for i in range(8):
                                W(t, "t", fst[("wrt", i)])
                                qq = q[6] if i < 4 else q[5]
                                ins = t.transpose(qq[0:8, (i % 4) * 128:(i % 4 + 1) * 128],
                                                  f_w[:, (p * 8 + i) * 8:(p * 8 + i + 1) * 8],
                                                  c_idf[:, :])
                            cT.inc(ins)
                        t_wtp = cT.tok()

                        @blk.vector
                        def _(v, p=p, t_wtp=t_wtp):
                            W(v, "v", t_wtp)
                            v.tensor_copy(f_wt[0:8, p * 1024:p * 1024 + 512], q[6][0:8, :])
                            ins = v.tensor_copy(f_wt[0:8, p * 1024 + 512:(p + 1) * 1024], q[5][0:8, :])
                            cV.inc(ins)
                        t_wtev = cV.tok()
                        fst["q_free"][6] = t_wtev
                        fst["q_free"][5] = t_wtev

                        for i in range(8):
                            @blk.tensor
                            def _(t, p=p, i=i):
                                W(t, "t", t_wtev, fst["q_free"][6], fst["q_free"][7])
                                t.matmul(q[6][:, :], f_wt[0:8, (p * 8 + i) * 128:(p * 8 + i + 1) * 128],
                                         c_b2[:, 0:512], start=True, stop=True)
                                ins = t.matmul(q[7][:, :], f_wt[0:8, (p * 8 + i) * 128:(p * 8 + i + 1) * 128],
                                               c_b2[:, 512:1024], start=True, stop=True)
                                cT.inc(ins)
                            t_b2 = cT.tok()

                            @blk.vector
                            def _(v, i=i, aslot=aslot, t_b2=t_b2):
                                W(v, "v", t_b2, fst["base_ld"])
                                a0 = aslot * 8 * D + i * D
                                v.tensor_tensor(f_acc[:, a0:a0 + 512], f_acc[:, a0:a0 + 512],
                                                q[6][:, :], ALU.add)
                                ins = v.tensor_tensor(f_acc[:, a0 + 512:a0 + 1024],
                                                      f_acc[:, a0 + 512:a0 + 1024], q[7][:, :], ALU.add)
                                cV.inc(ins)
                            tk = cV.tok()
                            fst["q_free"][6] = tk
                            fst["q_free"][7] = tk

                        for e in range(E):
                            emit_expert(p, e, aslot)

                        @blk.sync
                        def _(y, p=p, aslot=aslot):
                            W(y, "y", fst["evac_last"])
                            ins = y.dma_start(
                                bass.AP(out_h, p * 1024 * D, [[D, 128], [128 * D, 8], [1, D]]),
                                f_acc[:, aslot * 8 * D:(aslot + 1) * 8 * D])
                            fst["acc_free"][aslot] = dA[aslot].inc(ins)
                        fst["xnt_free"] = cT.tok()

                    def emit_expert(p, e, aslot):
                        cT0_a = cT.v
                        cS0_a = cS.v

                        @blk.gpsimd
                        def _(g, e=e):
                            for hs in range(32):
                                slot = hs % 4
                                if hs >= 4:
                                    WF(g, "g", cT, cT0_a + (hs - 4) * 2 + 2)
                                elif fst["w1_slot"][slot] is not None:
                                    W(g, "g", fst["w1_slot"][slot])
                                ins = g.dma_start(
                                    bass.AP(f_w1, slot * 1024, [[4 * 1024, 128], [128, 8], [1, 128]]),
                                    bass.AP(w1_h, (e * D) * HF + hs * 128,
                                            [[HF, 128], [128 * HF, 8], [1, 128]]))
                                fst[("w1d", hs)] = rW1[slot].inc(ins)
                                fst["w1_slot"][slot] = (cT, cT0_a + hs * 2 + 2)

                        @blk.tensor
                        def _(t):
                            for bkk in range(4):
                                W(t, "t", fst["q_free"][bkk])
                            for hs in range(32):
                                slot = hs % 4
                                W(t, "t", fst[("w1d", hs)])
                                for sub01 in range(2):
                                    bk = sub01 * 2 + hs % 2
                                    if hs >= 2:
                                        WF(t, "t", cS, cS0_a + (hs - 2) * 2 + sub01 + 1)
                                    for kd in range(8):
                                        ins = t.matmul(
                                            q[bk][:, :],
                                            f_w1[:, slot * 1024 + kd * 128:slot * 1024 + (kd + 1) * 128],
                                            f_xnt[:, sub01 * L + kd * 512:sub01 * L + (kd + 1) * 512],
                                            start=(kd == 0), stop=(kd == 7))
                                    cT.inc(ins)

                        @blk.scalar
                        def _(s, e=e):
                            for hs in range(32):
                                for sub01 in range(2):
                                    bk = sub01 * 2 + hs % 2
                                    WF(s, "s", cT, cT0_a + hs * 2 + sub01 + 1)
                                    ins = s.activation(
                                        f_ht[:, (sub01 * 32 + hs) * 512:(sub01 * 32 + hs + 1) * 512],
                                        q[bk][:, :], AF.Gelu,
                                        bias=c_b1[:, e * 32 + hs:e * 32 + hs + 1])
                                    cS.inc(ins)
                        t_ht_all = (cS, cS0_a + 64)
                        for bkk in range(8):
                            fst["q_free"][bkk] = t_ht_all

                        for dh in range(2):
                            for hs in range(32):
                                slot = (dh * 32 + hs) % 8

                                @blk.gpsimd
                                def _(g, e=e, dh=dh, hs=hs, slot=slot):
                                    W(g, "g", fst["w2_slot"][slot])
                                    ins = g.dma_start(
                                        bass.AP(f_w2, slot * 512, [[8 * 512, 128], [1, 512]]),
                                        bass.AP(w2_h, (e * HF + hs * 128) * D + dh * 512,
                                                [[D, 128], [1, 512]]))
                                    fst[("w2d", dh, hs)] = rW2[slot].inc(ins)

                                @blk.tensor
                                def _(t, dh=dh, hs=hs, slot=slot):
                                    W(t, "t", fst[("w2d", dh, hs)], t_ht_all)
                                    if hs == 0:
                                        for bkk in range(8):
                                            W(t, "t", fst["q_free"][bkk])
                                    for sub01 in range(2):
                                        for t128 in range(4):
                                            bk = sub01 * 4 + t128
                                            ins = t.matmul(
                                                q[bk][:, :],
                                                f_ht[:, (sub01 * 32 + hs) * 512 + t128 * 128:
                                                      (sub01 * 32 + hs) * 512 + (t128 + 1) * 128],
                                                f_w2[:, slot * 512:(slot + 1) * 512],
                                                start=(hs == 0), stop=(hs == 31))
                                    cT.inc(ins)
                                fst["w2_slot"][slot] = cT.tok()
                            t_bmm = cT.tok()

                            @blk.vector
                            def _(v, p=p, e=e, dh=dh, aslot=aslot, t_bmm=t_bmm):
                                W(v, "v", t_bmm)
                                for sub01 in range(2):
                                    for t128 in range(4):
                                        bk = sub01 * 4 + t128
                                        t128g = p * 8 + sub01 * 4 + t128
                                        a0 = aslot * 8 * D + (sub01 * 4 + t128) * D + dh * 512
                                        ins = v.scalar_tensor_tensor(
                                            f_acc[:, a0:a0 + 512], q[bk][:, :],
                                            f_w[:, t128g * 8 + e:t128g * 8 + e + 1],
                                            f_acc[:, a0:a0 + 512], ALU.mult, ALU.add)
                                        fst["q_free"][bk] = cV.inc(ins)
                            fst["evac_last"] = cV.tok()

                    for p in range(n_pairs):
                        emit_pair(p)

                    if "ffn" in debug:
                        @blk.sync
                        def _(y):
                            W(y, "y", cV.tok())
                            dX.inc(y.dma_start(dbg["lg"][:, 0:64], f_lg[:, :]))
                            dX.inc(y.dma_start(dbg["w"][:, :], f_w[:, :]))

                    @blk.sync
                    def _(y):
                        for chn in [dB[0], dB[1], dA[0], dA[1], dX]:
                            if chn.v > 0:
                                y.wait_ge(chn.sem, chn.v)

        else:
            with nc.Block() as blk:
                @blk.sync
                def _(y):
                    for chn in [dB[0], dB[1], dX]:
                        if chn.v > 0:
                            y.wait_ge(chn.sem, chn.v)

    return nc


_CACHE = {}


def _get_prog(key):
    if key not in _CACHE:
        n_sc, do_ffn, debug = key
        _CACHE[key] = build_program(n_sc=n_sc, do_ffn=do_ffn, debug=debug)
    return _CACHE[key]


def make_inputs(inputs):
    f32 = np.float32
    x = np.asarray(inputs["L0_attn_x"], f32)
    wipT = np.ascontiguousarray(np.asarray(inputs["L0_attn_in_proj_w"], f32).T)
    cw = np.asarray(inputs["L0_attn_conv_w"], f32)
    cdiag = np.zeros((NXT, 4, 128, 128), f32)
    ii = np.arange(128)
    cdiag[:, :, ii, ii] = cw.reshape(NXT, 128, 4).transpose(0, 2, 1)
    cdiag = cdiag.reshape(NXT * 4 * 128, 128)
    cbT = np.ascontiguousarray(np.asarray(inputs["L0_attn_conv_b"], f32).reshape(NXT, 128).T)
    dtb = np.ascontiguousarray(np.asarray(inputs["L0_attn_dt_bias"], f32).reshape(NH, 1))
    alog = np.ascontiguousarray(np.asarray(inputs["L0_attn_A_log"], f32).reshape(NH, 1))
    dbc = np.ascontiguousarray(np.broadcast_to(
        np.asarray(inputs["L0_attn_D"], f32).reshape(1, NH), (128, NH)))
    nwT = np.ascontiguousarray(np.asarray(inputs["L0_attn_norm_w"], f32).reshape(16, 128).T)
    wopT = np.ascontiguousarray(np.asarray(inputs["L0_attn_out_proj_w"], f32).T)
    w1 = np.ascontiguousarray(np.asarray(inputs["L0_ffn_W1"], f32).reshape(E * D, HF))
    b1T = np.ascontiguousarray(np.asarray(inputs["L0_ffn_b1"], f32)
                               .reshape(E, 32, 128).transpose(0, 2, 1)).reshape(E * 128, 32)
    w2 = np.ascontiguousarray(np.asarray(inputs["L0_ffn_W2"], f32).reshape(E * HF, D))
    b2 = np.ascontiguousarray(np.asarray(inputs["L0_ffn_b2"], f32))
    wr = np.ascontiguousarray(np.asarray(inputs["L0_ffn_Wr"], f32))
    br = np.ascontiguousarray(np.asarray(inputs["L0_ffn_br"], f32).reshape(1, E))

    s = np.arange(128)
    u01 = (s[:, None] <= s[None, :]).astype(f32)
    a1s = (s[:, None] > s[None, :]).astype(f32)
    idf = np.eye(128, dtype=f32)
    ones1 = np.ones((1, 128), f32)

    shared = dict(wipT=wipT, cdiag=cdiag, cbT=cbT, dtb=dtb, alog=alog, dbc=dbc,
                  nwT=nwT, wopT=wopT, w1=w1, b1T=b1T, w2=w2, b2=b2, wr=wr, br=br,
                  u01=u01, a1s=a1s, idf=idf, ones1=ones1)
    in_maps = []
    for b in range(8):
        d = dict(shared)
        d["x"] = np.ascontiguousarray(x[b])
        in_maps.append(d)
    return in_maps


def kernel(**inputs):
    nc = _get_prog((8, True, ()))
    in_maps = make_inputs(inputs)
    res = run_bass_kernel_spmd(nc, in_maps, core_ids=list(range(8)))
    return np.stack([np.asarray(r["out"], np.float32) for r in res.results], axis=0)


# version 07
# speedup vs baseline: 1.1897x; 1.0984x over previous
import numpy as np
import ml_dtypes
from contextlib import ExitStack
import concourse.bass as bass
import concourse.mybir as mybir
from concourse.bass_utils import run_bass_kernel_spmd

F32 = mybir.dt.float32
BF16 = mybir.dt.bfloat16
AF = mybir.ActivationFunctionType
ALU = mybir.AluOpType
AXX = mybir.AxisListType.X

L = 4096
D = 1024
DIN = 2048
NH = 32
HP = 64
NST = 128
DIP = 4384
NZT = 16
NXT = 18
XPW = 520
SCL = 512
NCH = 4
Q = 128
E = 8
HF = 4096


class Ch:
    def __init__(self, ctx, nc, name, step=1):
        self.sem = ctx.enter_context(nc.semaphore(name))
        self.v = 0
        self.step = step

    def inc(self, ins):
        self.v += self.step
        ins.then_inc(self.sem, self.step)
        return (self, self.v)

    def tok(self):
        return (self, self.v)


def build_program(n_sc=8, do_ffn=True, debug=(), n_pairs=4):
    nc = bass.Bass("TRN2", target_bir_lowering=False)

    x_h = nc.dram_tensor("x", [L, D], F32, kind="ExternalInput")
    wipT_h = nc.dram_tensor("wipT", [D, DIP], F32, kind="ExternalInput")
    cdiag_h = nc.dram_tensor("cdiag", [NXT * 4 * 128, 128], F32, kind="ExternalInput")
    cbT_h = nc.dram_tensor("cbT", [128, NXT], F32, kind="ExternalInput")
    dtb_h = nc.dram_tensor("dtb", [NH, 1], F32, kind="ExternalInput")
    alog_h = nc.dram_tensor("alog", [NH, 1], F32, kind="ExternalInput")
    dbc_h = nc.dram_tensor("dbc", [128, NH], F32, kind="ExternalInput")
    wopT_h = nc.dram_tensor("wopT", [DIN, D], F32, kind="ExternalInput")
    nwT_h = nc.dram_tensor("nwT", [128, 16], F32, kind="ExternalInput")
    w1_h = nc.dram_tensor("w1", [E * D, HF], F32, kind="ExternalInput")
    b1T_h = nc.dram_tensor("b1T", [E * 128, 32], F32, kind="ExternalInput")
    w2_h = nc.dram_tensor("w2", [E * HF, D], F32, kind="ExternalInput")
    b2_h = nc.dram_tensor("b2", [E, D], F32, kind="ExternalInput")
    wr_h = nc.dram_tensor("wr", [D, E], F32, kind="ExternalInput")
    br_h = nc.dram_tensor("br", [1, E], F32, kind="ExternalInput")
    u01_h = nc.dram_tensor("u01", [128, 128], F32, kind="ExternalInput")
    a1s_h = nc.dram_tensor("a1s", [128, 128], F32, kind="ExternalInput")
    idf_h = nc.dram_tensor("idf", [128, 128], F32, kind="ExternalInput")
    ones_h = nc.dram_tensor("ones1", [1, 128], F32, kind="ExternalInput")
    xnt_hbm = nc.dram_tensor("xnt_scr", [8 * 128, L], BF16)
    out_h = nc.dram_tensor("out", [L, D], F32, kind="ExternalOutput")

    dbg = {}
    if "zx" in debug:
        dbg["z"] = nc.dram_tensor("dbg_z", [8 * 128, NZT * SCL], BF16, kind="ExternalOutput")
        dbg["xbc"] = nc.dram_tensor("dbg_xbc", [8 * 128, NXT * XPW], BF16, kind="ExternalOutput")
        dbg["dt"] = nc.dram_tensor("dbg_dt", [8 * NH, SCL], F32, kind="ExternalOutput")
        dbg["sp"] = nc.dram_tensor("dbg_sp", [8 * NH, SCL], F32, kind="ExternalOutput")
        dbg["xn"] = nc.dram_tensor("dbg_xn", [8 * 128, NCH * D], BF16, kind="ExternalOutput")
    if "ssd" in debug:
        dbg["cumt"] = nc.dram_tensor("dbg_cumt", [L, NH], F32, kind="ExternalOutput")
        dbg["ytm"] = nc.dram_tensor("dbg_ytm", [L, DIN], F32, kind="ExternalOutput")
        dbg["yrms"] = nc.dram_tensor("dbg_yrms", [L, DIN], BF16, kind="ExternalOutput")
    if "ffn" in debug:
        dbg["lg"] = nc.dram_tensor("dbg_lg", [128, 256], F32, kind="ExternalOutput")
        dbg["w"] = nc.dram_tensor("dbg_w", [128, 256], F32, kind="ExternalOutput")

    ctx = ExitStack()
    with ctx:
        def sb(name, shape, dt=F32):
            return ctx.enter_context(nc.sbuf_tensor(name, shape, dt))

        def psum(name, shape, dt=F32):
            return ctx.enter_context(nc.psum_tensor(name, shape, dt))

        dSetS = Ch(ctx, nc, "dSetS", 16)
        dSetG = Ch(ctx, nc, "dSetG", 16)
        dWop = Ch(ctx, nc, "dWop", 16)
        dSpill = Ch(ctx, nc, "dSpill", 16)
        dXld = Ch(ctx, nc, "dXld", 16)
        dB = [Ch(ctx, nc, f"dB{i}", 16) for i in range(2)]
        dA = [Ch(ctx, nc, f"dA{i}", 16) for i in range(2)]
        dAL = Ch(ctx, nc, "dAL", 16)
        dXnt = Ch(ctx, nc, "dXnt", 16)
        dX = Ch(ctx, nc, "dX", 16)
        rWip = [Ch(ctx, nc, f"rWip{i}", 16) for i in range(4)]
        rW1 = [Ch(ctx, nc, f"rW1_{i}", 16) for i in range(4)]
        rW2 = [Ch(ctx, nc, f"rW2_{i}", 16) for i in range(8)]
        cT = Ch(ctx, nc, "cT")
        cV = Ch(ctx, nc, "cV")
        cS = Ch(ctx, nc, "cS")
        cG = Ch(ctx, nc, "cG")

        waited = {}

        def W(eng, key, *toks):
            hw = waited.setdefault(key, {})
            for t in toks:
                if t is None:
                    continue
                ch, v = t
                if v > 0 and v > hw.get(ch, 0):
                    eng.wait_ge(ch.sem, v)
                    hw[ch] = v

        def WF(eng, key, ch, v):
            hw = waited.setdefault(key, {})
            if v > 0 and v > hw.get(ch, 0):
                eng.wait_ge(ch.sem, v)
                hw[ch] = v

        c_u01 = sb("c_u01", [128, 128])
        c_a1 = sb("c_a1", [128, 128])
        c_a1b = sb("c_a1b", [128, 128], BF16)
        c_idf = sb("c_idf", [128, 128])
        c_idb = sb("c_idb", [128, 128], BF16)
        c_ones = sb("c_ones", [1, 128])
        c_onesb = sb("c_onesb", [1, 128], BF16)
        c_cbT = sb("c_cbT", [128, NXT])
        c_dtb = sb("c_dtb", [NH, 1])
        c_alog = sb("c_alog", [NH, 1])
        c_expa = sb("c_expa", [NH, 1])
        c_dbc = sb("c_dbc", [128, NH])
        c_nwT = sb("c_nwT", [128, 16])
        c_diag = sb("c_diag", [128, NXT * 4 * 128], BF16)
        c_wr = sb("c_wr", [128, 64], BF16)
        c_br = sb("c_br", [1, E], BF16)
        c_b2 = sb("c_b2", [E, D])
        c_eps = sb("c_eps", [128, 1])
        c_one = sb("c_one", [128, 1])
        c_b1 = sb("c_b1", [128, E * 32])

        with nc.Block() as blk:
            @blk.sync
            def _(y):
                for dst, src in [(c_u01, u01_h), (c_a1, a1s_h), (c_idf, idf_h),
                                 (c_ones, ones_h), (c_cbT, cbT_h), (c_dtb, dtb_h),
                                 (c_alog, alog_h), (c_dbc, dbc_h), (c_nwT, nwT_h),
                                 (c_b2, b2_h)]:
                    dSetS.inc(y.dma_start(dst[:, :], src[:, :]))
                dSetS.inc(y.dma_start(
                    bass.AP(c_b1, 0, [[E * 32, 128], [32, E], [1, 32]]),
                    bass.AP(b1T_h, 0, [[32, 128], [128 * 32, E], [1, 32]])))

            @blk.gpsimd
            def _(g):
                dSetG.inc(g.dma_start(c_a1b[:, :], a1s_h[:, :]))
                dSetG.inc(g.dma_start(c_idb[:, :], idf_h[:, :]))
                dSetG.inc(g.dma_start(c_onesb[:, :], ones_h[:, :]))
                dSetG.inc(g.dma_start(c_br[:, :], br_h[:, :]))
                dSetG.inc(g.dma_start(
                    bass.AP(c_wr, 0, [[64, 128], [8, 8], [1, 8]]),
                    bass.AP(wr_h, 0, [[8, 128], [128 * 8, 8], [1, 8]])))
                dSetG.inc(g.dma_start(
                    bass.AP(c_diag, 0, [[NXT * 4 * 128, 128], [128, NXT * 4], [1, 128]]),
                    bass.AP(cdiag_h, 0, [[128, 128], [128 * 128, NXT * 4], [1, 128]])))

            t_setup_x = dSetS.tok()
            t_setup_w = dSetG.tok()

            @blk.vector
            def _(v):
                v.memset(c_eps[:, :], 1e-5)
                cV.inc(v.memset(c_one[:, :], 1.0))

            @blk.scalar
            def _(s):
                W(s, "s", t_setup_x)
                cS.inc(s.activation(c_expa[:, :], c_alog[:, :], AF.Exp))

            @blk.sync
            def _(y):
                y.wait_ge(dSetS.sem, dSetS.v)

            @blk.gpsimd
            def _(g):
                g.wait_ge(dSetG.sem, dSetG.v)

        m_ctx = ExitStack()
        with m_ctx:
            def msb(name, shape, dt=F32):
                return m_ctx.enter_context(nc.sbuf_tensor(name, shape, dt))

            m_wop = msb("m_wop", [128, 16 * D], BF16)
            m_xnt = msb("m_xnt", [128, 8 * SCL], BF16)
            m_xtm = msb("m_xtm", [128, NCH * D])
            m_xntm = msb("m_xntm", [128, NCH * D], BF16)
            m_z = msb("m_z", [128, NZT * SCL], BF16)
            m_xbc = msb("m_xbc", [128, NXT * XPW], BF16)
            m_dtraw = msb("m_dtraw", [NH, SCL])
            m_sp = msb("m_sp", [NH, SCL])
            m_t1 = msb("m_t1", [NH, SCL])
            m_dta = m_dtraw
            m_wip = msb("m_wip", [128, 4 * 1024], BF16)
            m_s = msb("m_s", [128, NH * HP])
            m_sbf = msb("m_sbf", [128, NH * HP], BF16)
            k_xs = msb("k_xs", [128, DIN], BF16)
            k_xd = msb("k_xd", [128, DIN], BF16)
            k_xdr = msb("k_xdr", [128, DIN], BF16)
            k_btm = msb("k_btm", [128, 128], BF16)
            k_g2u = msb("k_g2u", [128, 128], BF16)
            k_a2 = msb("k_a2", [128, 1024], BF16)
            k_m2 = msb("k_m2", [128, 2 * 1024], BF16)
            k_gm = msb("k_gm", [128, 2 * 1024], BF16)
            k_ytm = msb("k_ytm", [128, DIN])
            k_yrms = msb("k_yrms", [128, DIN], BF16)
            k_yrt = msb("k_yrt", [128, DIN], BF16)
            k_dtat = msb("k_dtat", [128, NH])
            k_dtt = msb("k_dtt", [128, NH])
            k_cumt = msb("k_cumt", [128, NH])
            k_bcpq = msb("k_bcpq", [128, NH])
            k_r = msb("k_r", [128, NH])
            k_pt = msb("k_pt", [128, NH])
            k_rarg = msb("k_rarg", [128, NH])
            k_cumq = msb("k_cumq", [NH, 1])
            k_cumqr = msb("k_cumqr", [1, NH])
            k_pqr = msb("k_pqr", [1, NH])
            k_ln = msb("k_ln", [128, 24])
            k_ssq = msb("k_ssq", [128, 1])
            k_rstd = msb("k_rstd", [128, 1])
            b_base = msb("b_base", [128, 2 * D])

            p0 = m_ctx.enter_context(nc.psum_tensor("p0", [128, 512], F32))
            p1 = m_ctx.enter_context(nc.psum_tensor("p1", [128, 512], F32))
            pT1 = m_ctx.enter_context(nc.psum_tensor("pT1", [128, 1024], BF16))
            pT2 = m_ctx.enter_context(nc.psum_tensor("pT2", [128, 1024], BF16))
            pE = m_ctx.enter_context(nc.psum_tensor("pE", [128, 512], F32))
            pF = m_ctx.enter_context(nc.psum_tensor("pF", [128, 512], F32))
            pG = m_ctx.enter_context(nc.psum_tensor("pG", [128, 512], F32))
            pM = m_ctx.enter_context(nc.psum_tensor("pM", [128, 512], F32))

            with nc.Block() as blk:
                @blk.gpsimd
                def _(g):
                    for kt in range(16):
                        dWop.inc(g.dma_start(
                            m_wop[:, kt * D:(kt + 1) * D],
                            bass.AP(wopT_h, kt * 128 * D, [[D, 128], [1, D]])))
                t_wop_dma = dWop.tok()

                @blk.vector
                def _(v):
                    W(v, "v", t_wop_dma)
                    for kt in range(16):
                        ins = v.tensor_scalar(m_wop[:, kt * D:(kt + 1) * D],
                                              m_wop[:, kt * D:(kt + 1) * D],
                                              c_nwT[:, kt:kt + 1], None, ALU.mult)
                    cV.inc(ins)
                t_wop = cV.tok()

                @blk.vector
                def _(v):
                    v.memset(m_s[:, :], 0.0)
                    v.memset(m_sbf[:, :], 0.0)
                    ins = v.memset(m_xbc[:, :], 0.0)
                    cV.inc(ins)
                t_minit = cV.tok()

                st = {
                    "xtm_free": None, "xntm_free": None, "xnt_free": None,
                    "z_free": None, "xbc_free": [None] * NXT,
                    "pT_free": [None, None], "pM_free": None,
                    "p01_free": [t_minit, t_minit],
                    "pE_free": t_minit, "pF_free": t_minit, "pG_free": t_minit,
                    "wip_slot": [None] * 4,
                    "a2_free": None, "m2_free": [None, None],
                    "gm_free": [None, None],
                    "state_w": t_minit, "base_free": [None, None],
                    "dt_free": None,
                }
                wip_f_global = [0]

                def emit_sc(sc):
                    @blk.sync
                    def _(y):
                        W(y, "y", st["xtm_free"])
                        t = dXld.inc(y.dma_start(
                            m_xtm[:, :],
                            bass.AP(x_h, sc * SCL * D, [[D, 128], [128 * D, NCH], [1, D]])))
                        st["x_ld"] = t

                    @blk.vector
                    def _(v):
                        W(v, "v", st["x_ld"], st["xntm_free"], st.get("yrms_rd"))
                        for t4 in range(NCH):
                            xs_ = m_xtm[:, t4 * D:(t4 + 1) * D]
                            col = lambda j: k_ln[:, t4 * 6 + j:t4 * 6 + j + 1]
                            v.reduce_sum(col(0), xs_, axis=AXX)
                            v.scalar_tensor_tensor(k_yrms[:, 0:1024], xs_, 1.0, xs_,
                                                   ALU.mult, ALU.mult, accum_out=col(2))
                            v.drain()
                            v.tensor_scalar(col(1), col(0), -1.0 / D, None, ALU.mult)
                            v.drain()
                            v.tensor_tensor(col(3), col(1), col(1), ALU.mult)
                            v.drain()
                            ins = v.scalar_tensor_tensor(col(4), col(2), 1.0 / D, col(3),
                                                         ALU.mult, ALU.subtract)
                            cV.inc(ins)
                    t_var = cV.tok()

                    @blk.scalar
                    def _(s):
                        W(s, "s", t_var)
                        for t4 in range(NCH):
                            col = lambda j: k_ln[:, t4 * 6 + j:t4 * 6 + j + 1]
                            s.activation(col(5), col(4), AF.Ln, bias=c_eps[:, :])
                            s.drain()
                            ins = s.activation(col(5), col(5), AF.Exp, scale=-0.5)
                        cS.inc(ins)
                    t_rstd = cS.tok()

                    @blk.vector
                    def _(v):
                        W(v, "v", t_rstd)
                        for t4 in range(NCH):
                            xs_ = m_xtm[:, t4 * D:(t4 + 1) * D]
                            col = lambda j: k_ln[:, t4 * 6 + j:t4 * 6 + j + 1]
                            ins = v.tensor_scalar(m_xntm[:, t4 * D:(t4 + 1) * D],
                                                  xs_, col(1), col(5), ALU.add, ALU.mult)
                            cV.inc(ins)
                            st[("ln", t4)] = cV.tok()

                    for t4 in range(NCH):
                        pt = pT1 if t4 % 2 == 0 else pT2

                        @blk.tensor
                        def _(t, t4=t4, pt=pt):
                            W(t, "t", st[("ln", t4)], st["pT_free"][t4 % 2], st["xnt_free"])
                            for kd in range(8):
                                ins = t.transpose(
                                    pt[:, kd * 128:(kd + 1) * 128],
                                    m_xntm[:, t4 * D + kd * 128:t4 * D + (kd + 1) * 128],
                                    c_idb[:, :])
                            cT.inc(ins)
                        t_tp = cT.tok()

                        @blk.vector
                        def _(v, t4=t4, pt=pt, t_tp=t_tp):
                            W(v, "v", t_tp)
                            ins = v.tensor_copy(
                                bass.AP(m_xnt, t4 * 128, [[8 * SCL, 128], [SCL, 8], [1, 128]]),
                                bass.AP(pt, 0, [[1024, 128], [128, 8], [1, 128]]))
                            cV.inc(ins)
                        st["pT_free"][t4 % 2] = cV.tok()
                        st[("xnt", t4)] = cV.tok()

                    t_xnt_all = cV.tok()

                    @blk.scalar
                    def _(s):
                        W(s, "s", t_xnt_all)
                        t = dSpill.inc(s.dma_start(
                            bass.AP(xnt_hbm, sc * 128 * L, [[L, 128], [1, 8 * SCL]]),
                            m_xnt[:, :]))
                        st["spill"] = t

                    if "zx" in debug:
                        @blk.sync
                        def _(y):
                            W(y, "y", t_xnt_all)
                            dX.inc(y.dma_start(
                                bass.AP(dbg["xn"], sc * 128 * (NCH * D), [[NCH * D, 128], [1, NCH * D]]),
                                m_xntm[:, :]))

                    cT0_ip = cT.v
                    cS0_ip = cS.v

                    @blk.gpsimd
                    def _(g):
                        for f in range(35):
                            slot = f % 4
                            wd = 128 if f < 34 else 32
                            tok = st["wip_slot"][slot]
                            if f >= 4:
                                WF(g, "g", cT, cT0_ip + f - 4 + 1)
                            elif tok is not None:
                                W(g, "g", tok)
                            ins = g.dma_start(
                                bass.AP(m_wip, slot * 1024, [[4 * 1024, 128], [wd, 8], [1, wd]]),
                                bass.AP(wipT_h, f * 128, [[DIP, 128], [128 * DIP, 8], [1, wd]]))
                            st[("wipd", f)] = rWip[slot].inc(ins)
                            st["wip_slot"][slot] = (cT, cT0_ip + f + 1)
                            wip_f_global[0] += 1

                    @blk.tensor
                    def _(t):
                        W(t, "t", t_xnt_all, t_wop, st["p01_free"][0], st["p01_free"][1])
                        for f in range(35):
                            slot = f % 4
                            wd = 128 if f < 34 else 32
                            pp = p0 if f % 2 == 0 else p1
                            W(t, "t", st[("wipd", f)])
                            if f >= 2:
                                WF(t, "t", cS, cS0_ip + f - 2 + 1)
                            for kd in range(8):
                                ins = t.matmul(pp[:wd, :],
                                               m_wip[:, slot * 1024 + kd * wd:slot * 1024 + (kd + 1) * wd],
                                               m_xnt[:, kd * SCL:(kd + 1) * SCL],
                                               start=(kd == 0), stop=(kd == 7))
                            cT.inc(ins)

                    @blk.scalar
                    def _(s):
                        for f in range(35):
                            pp = p0 if f % 2 == 0 else p1
                            WF(s, "s", cT, cT0_ip + f + 1)
                            if f < NZT:
                                ins = s.activation(m_z[:, f * SCL:(f + 1) * SCL], pp[:, :], AF.Silu)
                            elif f < 34:
                                ct = f - NZT
                                ins = s.activation(
                                    bass.AP(m_xbc, ct * XPW + 4, [[NXT * XPW, 128], [1, SCL]]),
                                    pp[:, :], AF.Copy)
                            else:
                                ins = s.activation(m_dtraw[:, :], pp[:NH, :], AF.Copy)
                            cS.inc(ins)
                            st[("ipev", f)] = cS.tok()
                    st["p01_free"][0] = (cS, cS0_ip + 35)
                    st["p01_free"][1] = (cS, cS0_ip + 35)

                    @blk.scalar
                    def _(s):
                        W(s, "s", st[("ipev", 34)])
                        s.activation(m_t1[:, :], m_dtraw[:, :], AF.Abs, bias=c_dtb[:, :])
                        s.drain()
                        s.activation(m_t1[:, :], m_t1[:, :], AF.Exp, scale=-1.0)
                        s.drain()
                        s.activation(m_t1[:, :], m_t1[:, :], AF.Ln, bias=c_one[0:NH, :])
                        s.drain()
                        ins = s.activation(m_sp[:, :], m_dtraw[:, :], AF.Relu, bias=c_dtb[:, :])
                        cS.inc(ins)
                    t_sp1 = cS.tok()

                    @blk.vector
                    def _(v):
                        W(v, "v", t_sp1, t_setup_x)
                        v.tensor_tensor(m_sp[:, :], m_sp[:, :], m_t1[:, :], ALU.add)
                        ins = v.tensor_scalar(m_dta[:, :], m_sp[:, :], c_expa[:, :], -1.0,
                                              ALU.mult, ALU.mult)
                        cV.inc(ins)
                    st["dtA"] = cV.tok()
                    st["dt_free"] = None

                    cT0_cv = cT.v
                    cS0_cv = cS.v

                    @blk.tensor
                    def _(t):
                        W(t, "t", t_setup_w, st["p01_free"][0], st["p01_free"][1])
                        for ct in range(NXT):
                            pp = p0 if ct % 2 == 0 else p1
                            W(t, "t", st[("ipev", 16 + ct)])
                            if ct >= 2:
                                WF(t, "t", cS, cS0_cv + ct - 2 + 1)
                            for k in range(4):
                                ins = t.matmul(
                                    pp[:, :],
                                    c_diag[:, (ct * 4 + k) * 128:(ct * 4 + k + 1) * 128],
                                    bass.AP(m_xbc, ct * XPW + 1 + k, [[NXT * XPW, 128], [1, SCL]]),
                                    start=(k == 0), stop=(k == 3))
                            cT.inc(ins)

                    @blk.vector
                    def _(v):
                        for ct in range(NXT):
                            WF(v, "v", cT, cT0_cv + ct + 1)
                            ins = v.tensor_copy(
                                bass.AP(m_xbc, ct * XPW + 1, [[NXT * XPW, 128], [1, 3]]),
                                bass.AP(m_xbc, ct * XPW + 513, [[NXT * XPW, 128], [1, 3]]))
                            cV.inc(ins)
                    cV0_cv = cV.v

                    @blk.scalar
                    def _(s):
                        for ct in range(NXT):
                            pp = p0 if ct % 2 == 0 else p1
                            WF(s, "s", cV, cV0_cv - NXT + ct + 1)
                            ins = s.activation(
                                bass.AP(m_xbc, ct * XPW + 4, [[NXT * XPW, 128], [1, SCL]]),
                                pp[:, :], AF.Silu, bias=c_cbT[:, ct:ct + 1])
                            cS.inc(ins)
                            st[("cv", ct)] = cS.tok()
                    st["p01_free"][0] = (cS, cS0_cv + NXT)
                    st["p01_free"][1] = (cS, cS0_cv + NXT)

                    t_conv_all = cS.tok()

                    if "zx" in debug:
                        @blk.sync
                        def _(y):
                            W(y, "y", st[("ipev", 15)], t_conv_all, st["dtA"])
                            dX.inc(y.dma_start(
                                bass.AP(dbg["z"], sc * 128 * (NZT * SCL), [[NZT * SCL, 128], [1, NZT * SCL]]),
                                m_z[:, :]))
                            dX.inc(y.dma_start(
                                bass.AP(dbg["xbc"], sc * 128 * (NXT * XPW), [[NXT * XPW, 128], [1, NXT * XPW]]),
                                m_xbc[:, :]))
                            dX.inc(y.dma_start(
                                bass.AP(dbg["dt"], sc * NH * SCL, [[SCL, NH], [1, SCL]]),
                                m_dtraw[:, :]))
                            dX.inc(y.dma_start(
                                bass.AP(dbg["sp"], sc * NH * SCL, [[SCL, NH], [1, SCL]]),
                                m_sp[:, :]))

                    for c in range(NCH):
                        emit_chunk(sc, c)

                    st["xtm_free"] = cV.tok()
                    st["xntm_free"] = cV.tok()
                    st["xnt_free"] = st["spill"]
                    st["z_free"] = cV.tok()
                    for ct in range(NXT):
                        st["xbc_free"][ct] = cV.tok()
                    st["dt_free"] = cV.tok()

                def emit_chunk(sc, c):
                    co = c * Q
                    @blk.vector
                    def _(v):
                        W(v, "v", st["dtA"])
                        ins = v.reduce_sum(k_cumq[:, :], m_dta[:, co:co + Q], axis=AXX)
                        cV.inc(ins)
                    t_cumq = cV.tok()

                    @blk.tensor
                    def _(t):
                        W(t, "t", t_cumq, st["pM_free"])
                        t.transpose(pM[:, 0:32], m_dta[:NH, co:co + Q], c_idf[0:NH, 0:NH])
                        t.transpose(pM[:, 32:64], m_sp[:NH, co:co + Q], c_idf[0:NH, 0:NH])
                        ins = t.transpose(pM[0:1, 64:96], k_cumq[:, :], c_idf[0:NH, 0:NH])
                        cT.inc(ins)
                    t_rA = cT.tok()

                    @blk.vector
                    def _(v):
                        W(v, "v", t_rA)
                        v.tensor_copy(k_dtat[:, :], pM[:, 0:32])
                        v.tensor_copy(k_dtt[:, :], pM[:, 32:64])
                        ins = v.tensor_copy(k_cumqr[:, :], pM[0:1, 64:96])
                        cV.inc(ins)
                    t_rAev = cV.tok()

                    @blk.scalar
                    def _(s):
                        W(s, "s", t_rAev)
                        ins = s.activation(k_pqr[:, :], k_cumqr[:, :], AF.Exp)
                        cS.inc(ins)
                    t_pqr = cS.tok()

                    @blk.tensor
                    def _(t):
                        W(t, "t", t_rAev, t_pqr)
                        t.matmul(pM[:, 0:32], c_u01[:, :], k_dtat[:, :], start=True, stop=True)
                        t.matmul(pM[:, 32:64], c_ones[:, :], k_cumqr[:, :], start=True, stop=True)
                        ins = t.matmul(pM[:, 64:96], c_ones[:, :], k_pqr[:, :], start=True, stop=True)
                        cT.inc(ins)
                    t_rB = cT.tok()

                    @blk.vector
                    def _(v):
                        W(v, "v", t_rB)
                        v.tensor_copy(k_cumt[:, :], pM[:, 0:32])
                        v.drain()
                        v.tensor_tensor(k_rarg[:, :], pM[:, 32:64], k_cumt[:, :], ALU.subtract)
                        ins = v.tensor_copy(k_bcpq[:, :], pM[:, 64:96])
                        cV.inc(ins)
                    st["pM_free"] = cV.tok()
                    t_rBev = cV.tok()

                    @blk.scalar
                    def _(s):
                        W(s, "s", t_rBev)
                        s.activation(k_pt[:, :], k_cumt[:, :], AF.Exp)
                        ins = s.activation(k_r[:, :], k_rarg[:, :], AF.Exp)
                        cS.inc(ins)
                    t_ptr = cS.tok()

                    if "ssd" in debug:
                        @blk.sync
                        def _(y):
                            W(y, "y", t_rBev)
                            dX.inc(y.dma_start(
                                bass.AP(dbg["cumt"], (sc * NCH + c) * 128 * NH, [[NH, 128], [1, NH]]),
                                k_cumt[:, :]))

                    @blk.tensor
                    def _(t):
                        W(t, "t", st[("cv", 16)], st[("cv", 0)], st[("cv", 6)],
                          st["pT_free"][0], st["pT_free"][1])
                        for j in range(7):
                            W(t, "t", st[("cv", j)])
                        t.transpose(pT1[:, 0:128],
                                    bass.AP(m_xbc, 16 * XPW + 4 + co, [[NXT * XPW, 128], [1, Q]]),
                                    c_idb[:, :])
                        for j in range(7):
                            ins = t.transpose(pT1[:, (j + 1) * 128:(j + 2) * 128],
                                              bass.AP(m_xbc, j * XPW + 4 + co, [[NXT * XPW, 128], [1, Q]]),
                                              c_idb[:, :])
                        cT.inc(ins)
                    t_tpa = cT.tok()

                    @blk.tensor
                    def _(t):
                        for j in range(7, 15):
                            W(t, "t", st[("cv", j)])
                        for j in range(7, 15):
                            ins = t.transpose(pT2[:, (j - 7) * 128:(j - 6) * 128],
                                              bass.AP(m_xbc, j * XPW + 4 + co, [[NXT * XPW, 128], [1, Q]]),
                                              c_idb[:, :])
                        cT.inc(ins)
                    t_tpb = cT.tok()

                    @blk.vector
                    def _(v):
                        W(v, "v", t_tpa, t_tpb)
                        v.tensor_copy(k_btm[:, :], pT1[:, 0:128])
                        v.tensor_copy(k_xs[:, 0:7 * 128], pT1[:, 128:1024])
                        ins = v.tensor_copy(k_xs[:, 7 * 128:15 * 128], pT2[:, :])
                        cV.inc(ins)
                    t_ev1 = cV.tok()
                    st["pT_free"][0] = t_ev1
                    st["pT_free"][1] = t_ev1

                    @blk.tensor
                    def _(t):
                        W(t, "t", st[("cv", 15)], t_ev1)
                        ins = t.transpose(pT1[:, 0:128],
                                          bass.AP(m_xbc, 15 * XPW + 4 + co, [[NXT * XPW, 128], [1, Q]]),
                                          c_idb[:, :])
                        cT.inc(ins)
                    t_tpc = cT.tok()

                    @blk.vector
                    def _(v):
                        W(v, "v", t_tpc)
                        ins = v.tensor_copy(k_xs[:, 15 * 128:DIN], pT1[:, 0:128])
                        cV.inc(ins)
                    st["pT_free"][0] = cV.tok()
                    t_xs = cV.tok()

                    @blk.vector
                    def _(v):
                        W(v, "v", t_xs, t_ptr)
                        for h in range(NH):
                            v.tensor_scalar(k_xd[:, h * HP:(h + 1) * HP],
                                            k_xs[:, h * HP:(h + 1) * HP],
                                            k_dtt[:, h:h + 1], None, ALU.mult)
                        for h in range(NH):
                            ins = v.tensor_scalar(k_xdr[:, h * HP:(h + 1) * HP],
                                                  k_xd[:, h * HP:(h + 1) * HP],
                                                  k_r[:, h:h + 1], None, ALU.mult)
                        cV.inc(ins)
                    t_xdr = cV.tok()

                    @blk.tensor
                    def _(t):
                        W(t, "t", st[("cv", 16)], st[("cv", 17)], st["pE_free"])
                        ins = t.matmul(pE[:, 0:128],
                                       bass.AP(m_xbc, 16 * XPW + 4 + co, [[NXT * XPW, 128], [1, Q]]),
                                       bass.AP(m_xbc, 17 * XPW + 4 + co, [[NXT * XPW, 128], [1, Q]]),
                                       start=True, stop=True)
                        cT.inc(ins)
                    t_g2 = cT.tok()

                    @blk.vector
                    def _(v):
                        W(v, "v", t_g2)
                        ins = v.tensor_tensor(k_g2u[:, :], pE[:, 0:128], c_u01[:, :], ALU.mult)
                        cV.inc(ins)
                    t_g2u = cV.tok()
                    st["pE_free"] = t_g2u

                    for g in range(4):
                        gb = g % 2
                        hh0 = g * 8

                        @blk.vector
                        def _(v, g=g, gb=gb, hh0=hh0):
                            W(v, "v", t_rAev, st["a2_free"])
                            for i in range(8):
                                ins = v.tensor_scalar(k_a2[:, i * 128:(i + 1) * 128],
                                                      c_u01[:, :], k_dtat[:, hh0 + i:hh0 + i + 1],
                                                      None, ALU.mult)
                            cV.inc(ins)
                        t_a2 = cV.tok()

                        @blk.tensor
                        def _(t, g=g, gb=gb, t_a2=t_a2):
                            W(t, "t", t_a2, st["p01_free"][0], st["p01_free"][1])
                            for i in range(8):
                                pp = p0 if i < 4 else p1
                                ins = t.matmul(pp[:, (i % 4) * 128:(i % 4 + 1) * 128],
                                               c_a1b[:, :],
                                               k_a2[:, i * 128:(i + 1) * 128],
                                               start=True, stop=True)
                            cT.inc(ins)
                        t_d2 = cT.tok()
                        st["a2_free"] = t_d2

                        @blk.scalar
                        def _(s, gb=gb, t_d2=t_d2):
                            W(s, "s", t_d2, st["m2_free"][gb])
                            s.activation(k_m2[:, gb * 1024:gb * 1024 + 512], p0[:, :], AF.Exp)
                            ins = s.activation(k_m2[:, gb * 1024 + 512:gb * 1024 + 1024], p1[:, :], AF.Exp)
                            cS.inc(ins)
                        t_m2 = cS.tok()
                        st["p01_free"][0] = t_m2
                        st["p01_free"][1] = t_m2

                        @blk.vector
                        def _(v, gb=gb, t_m2=t_m2):
                            W(v, "v", t_m2, t_g2u, st["gm_free"][gb])
                            for i in range(8):
                                ins = v.tensor_tensor(
                                    k_gm[:, gb * 1024 + i * 128:gb * 1024 + (i + 1) * 128],
                                    k_g2u[:, :],
                                    k_m2[:, gb * 1024 + i * 128:gb * 1024 + (i + 1) * 128],
                                    ALU.mult)
                            cV.inc(ins)
                        t_gm = cV.tok()
                        st["m2_free"][gb] = t_gm

                        @blk.tensor
                        def _(t, g=g, gb=gb, hh0=hh0, t_gm=t_gm):
                            W(t, "t", t_gm, t_xdr, st["state_w"], st.get(("sbf_prev", g)),
                              st["pE_free"], st["pF_free"], st["pG_free"])
                            for i in range(8):
                                h = hh0 + i
                                t.matmul(pE[:, i * HP:(i + 1) * HP],
                                         k_gm[:, gb * 1024 + i * 128:gb * 1024 + (i + 1) * 128],
                                         k_xd[:, h * HP:(h + 1) * HP], start=True, stop=True)
                            for i in range(8):
                                h = hh0 + i
                                t.matmul(pF[:, i * HP:(i + 1) * HP],
                                         bass.AP(m_xbc, 17 * XPW + 4 + co, [[NXT * XPW, 128], [1, Q]]),
                                         m_sbf[:, h * HP:(h + 1) * HP], start=True, stop=True)
                            for i in range(8):
                                h = hh0 + i
                                ins = t.matmul(pG[:, i * HP:(i + 1) * HP],
                                               k_btm[:, :],
                                               k_xdr[:, h * HP:(h + 1) * HP], start=True, stop=True)
                            cT.inc(ins)
                        t_y = cT.tok()
                        st["gm_free"][gb] = t_y

                        @blk.vector
                        def _(v, g=g, hh0=hh0, t_y=t_y):
                            W(v, "v", t_y, t_ptr, st.get(("sbf_prev", g)))
                            v.tensor_copy(k_ytm[:, g * 512:(g + 1) * 512], pE[:, :])
                            for i in range(8):
                                h = hh0 + i
                                sl = k_ytm[:, h * HP:(h + 1) * HP]
                                v.scalar_tensor_tensor(sl, pF[:, i * HP:(i + 1) * HP],
                                                       k_pt[:, h:h + 1], sl, ALU.mult, ALU.add)
                            for i in range(8):
                                h = hh0 + i
                                sl = k_ytm[:, h * HP:(h + 1) * HP]
                                v.scalar_tensor_tensor(sl, k_xs[:, h * HP:(h + 1) * HP],
                                                       c_dbc[:, h:h + 1], sl, ALU.mult, ALU.add)
                            for i in range(8):
                                h = hh0 + i
                                sl = m_s[:, h * HP:(h + 1) * HP]
                                v.scalar_tensor_tensor(sl, sl, k_bcpq[:, h:h + 1],
                                                       pG[:, i * HP:(i + 1) * HP],
                                                       ALU.mult, ALU.add)
                            for i in range(8):
                                h = hh0 + i
                                ins = v.tensor_copy(m_sbf[:, h * HP:(h + 1) * HP],
                                                    m_s[:, h * HP:(h + 1) * HP])
                            cV.inc(ins)
                        t_yev = cV.tok()
                        st[("sbf", g)] = t_yev
                        st["pE_free"] = t_yev
                        st["pF_free"] = t_yev
                        st["pG_free"] = t_yev
                    st["state_w"] = cV.tok()
                    for g in range(4):
                        st[("sbf_prev", g)] = st.get(("sbf", g))

                    @blk.tensor
                    def _(t):
                        W(t, "t", st[("ipev", 15)], st["pT_free"][0], st["pT_free"][1],
                          st["state_w"])
                        for j in range(8):
                            t.transpose(pT1[:, j * 128:(j + 1) * 128],
                                        m_z[:, j * SCL + co:j * SCL + co + Q], c_idb[:, :])
                        for j in range(8, 16):
                            ins = t.transpose(pT2[:, (j - 8) * 128:(j - 7) * 128],
                                              m_z[:, j * SCL + co:j * SCL + co + Q], c_idb[:, :])
                        cT.inc(ins)
                    t_ztp = cT.tok()

                    @blk.vector
                    def _(v):
                        W(v, "v", t_ztp)
                        v.tensor_tensor(k_ytm[:, 0:1024], k_ytm[:, 0:1024], pT1[:, :], ALU.mult)
                        ins = v.tensor_tensor(k_ytm[:, 1024:2048], k_ytm[:, 1024:2048], pT2[:, :], ALU.mult)
                        cV.inc(ins)
                    t_gate = cV.tok()
                    st["pT_free"][0] = t_gate
                    st["pT_free"][1] = t_gate

                    if "ssd" in debug:
                        @blk.sync
                        def _(y):
                            W(y, "y", t_gate)
                            dX.inc(y.dma_start(
                                bass.AP(dbg["ytm"], (sc * NCH + c) * 128 * DIN, [[DIN, 128], [1, DIN]]),
                                k_ytm[:, :]))

                    @blk.vector
                    def _(v):
                        W(v, "v", t_gate)
                        ins = v.scalar_tensor_tensor(k_yrms[:, :], k_ytm[:, :], 1.0, k_ytm[:, :],
                                                     ALU.mult, ALU.mult, accum_out=k_ssq[:, :])
                        cV.inc(ins)
                    t_ssq = cV.tok()

                    @blk.scalar
                    def _(s):
                        W(s, "s", t_ssq)
                        s.activation(k_rstd[:, :], k_ssq[:, :], AF.Ln, bias=c_eps[:, :], scale=1.0 / DIN)
                        s.drain()
                        ins = s.activation(k_rstd[:, :], k_rstd[:, :], AF.Exp, scale=-0.5)
                        cS.inc(ins)
                    t_rstd = cS.tok()

                    @blk.vector
                    def _(v):
                        W(v, "v", t_rstd)
                        ins = v.tensor_scalar(k_yrms[:, :], k_ytm[:, :], k_rstd[:, :], None, ALU.mult)
                        cV.inc(ins)
                    t_yrms = cV.tok()

                    if "ssd" in debug:
                        @blk.sync
                        def _(y):
                            W(y, "y", t_yrms)
                            dX.inc(y.dma_start(
                                bass.AP(dbg["yrms"], (sc * NCH + c) * 128 * DIN, [[DIN, 128], [1, DIN]]),
                                k_yrms[:, :]))

                    @blk.tensor
                    def _(t):
                        W(t, "t", t_yrms, st["pT_free"][0], st["pT_free"][1])
                        for kt in range(8):
                            t.transpose(pT1[:, kt * 128:(kt + 1) * 128],
                                        k_yrms[:, kt * 128:(kt + 1) * 128], c_idb[:, :])
                        for kt in range(8, 16):
                            ins = t.transpose(pT2[:, (kt - 8) * 128:(kt - 7) * 128],
                                              k_yrms[:, kt * 128:(kt + 1) * 128], c_idb[:, :])
                        cT.inc(ins)
                    t_ytp = cT.tok()
                    st["yrms_rd"] = t_ytp

                    @blk.vector
                    def _(v):
                        W(v, "v", t_ytp)
                        v.tensor_copy(k_yrt[:, 0:1024], pT1[:, :])
                        ins = v.tensor_copy(k_yrt[:, 1024:2048], pT2[:, :])
                        cV.inc(ins)
                    t_yrt = cV.tok()
                    st["pT_free"][0] = t_yrt
                    st["pT_free"][1] = t_yrt

                    @blk.tensor
                    def _(t):
                        W(t, "t", t_yrt, st["pE_free"], st["pF_free"], t_wop)
                        for kt in range(16):
                            t.matmul(pE[:, :], k_yrt[:, kt * 128:(kt + 1) * 128],
                                     m_wop[:, kt * D:kt * D + 512],
                                     start=(kt == 0), stop=(kt == 15))
                        for kt in range(16):
                            ins = t.matmul(pF[:, :], k_yrt[:, kt * 128:(kt + 1) * 128],
                                           m_wop[:, kt * D + 512:(kt + 1) * D],
                                           start=(kt == 0), stop=(kt == 15))
                        cT.inc(ins)
                    t_op = cT.tok()

                    bslot = c % 2

                    @blk.vector
                    def _(v, c=c, bslot=bslot, t_op=t_op):
                        W(v, "v", t_op, st["base_free"][bslot])
                        bb = b_base[:, bslot * D:(bslot + 1) * D]
                        v.tensor_tensor(bb, m_xtm[:, c * D:(c + 1) * D],
                                        m_xntm[:, c * D:(c + 1) * D], ALU.add)
                        v.tensor_tensor(b_base[:, bslot * D:bslot * D + 512],
                                        b_base[:, bslot * D:bslot * D + 512], pE[:, :], ALU.add)
                        ins = v.tensor_tensor(b_base[:, bslot * D + 512:(bslot + 1) * D],
                                              b_base[:, bslot * D + 512:(bslot + 1) * D], pF[:, :], ALU.add)
                        cV.inc(ins)
                    t_base = cV.tok()
                    st["pE_free"] = t_base
                    st["pF_free"] = t_base

                    @blk.sync
                    def _(y, sc=sc, c=c, bslot=bslot, t_base=t_base):
                        W(y, "y", t_base)
                        ins = y.dma_start(
                            bass.AP(out_h, (sc * SCL + c * Q) * D, [[D, 128], [1, D]]),
                            b_base[:, bslot * D:(bslot + 1) * D])
                        st["base_free"][bslot] = dB[bslot].inc(ins)

                for sc in range(n_sc):
                    emit_sc(sc)

                t_base_all = None
                t_base_all2 = [dB[0].tok(), dB[1].tok()]
                t_spill_all = dSpill.tok()

                @blk.sync
                def _(y):
                    W(y, "y", *t_base_all2)

        if do_ffn:
            f_ctx = ExitStack()
            with f_ctx:
                def fsb(name, shape, dt=F32):
                    return f_ctx.enter_context(nc.sbuf_tensor(name, shape, dt))

                f_xnt = fsb("f_xnt", [128, 2 * L], BF16)
                f_ht = fsb("f_ht", [128, 64 * 512], BF16)
                f_acc = fsb("f_acc", [128, 2 * 8 * D])
                f_w1 = fsb("f_w1", [128, 4 * 1024], BF16)
                f_w2 = fsb("f_w2", [128, 8 * 512], BF16)
                f_lg = fsb("f_lg", [128, 64])
                f_w = fsb("f_w", [128, 256])
                f_wt = fsb("f_wt", [8, L // 128 * 128])
                f_sm = fsb("f_sm", [128, 64])
                f_s8a = fsb("f_s8a", [128, 64])
                f_s8b = fsb("f_s8b", [128, 64])

                q = [f_ctx.enter_context(nc.psum_tensor(f"q{i}", [128, 512], F32))
                     for i in range(8)]

                with nc.Block() as blk:
                    fst = {
                        "q_free": [None] * 8,
                        "w1_slot": [None] * 4,
                        "w2_slot": [None] * 8,
                        "acc_free": [None, None],
                        "xnt_free": None,
                        "ht_tok": None,
                    }

                    def emit_pair(p):
                        aslot = p % 2
                        @blk.scalar
                        def _(s, p=p):
                            W(s, "s", t_spill_all, fst["xnt_free"])
                            for sub01 in range(2):
                                sub = p * 2 + sub01
                                ins = s.dma_start(
                                    f_xnt[:, sub01 * L:sub01 * L + L],
                                    bass.AP(xnt_hbm, sub * 128 * L, [[L, 128], [1, L]]))
                                fst[("xntd", sub01)] = dXnt.inc(ins)

                        @blk.sync
                        def _(y, p=p, aslot=aslot):
                            W(y, "y", *t_base_all2)
                            W(y, "y", fst["acc_free"][aslot])
                            ins = y.dma_start(
                                f_acc[:, aslot * 8 * D:(aslot + 1) * 8 * D],
                                bass.AP(out_h, p * 1024 * D, [[D, 128], [128 * D, 8], [1, D]]))
                            fst["base_ld"] = dAL.inc(ins)

                        @blk.tensor
                        def _(t, p=p):
                            W(t, "t", fst[("xntd", 0)], fst[("xntd", 1)], t_setup_w, t_setup_x)
                            W(t, "t", fst["q_free"][7])
                            for i in range(8):
                                sub01, t128 = i // 4, i % 4
                                t.matmul(q[7][:, i * 8:(i + 1) * 8], c_onesb[:, :], c_br[:, :],
                                         start=True, stop=False)
                                for kd in range(8):
                                    ins = t.matmul(
                                        q[7][:, i * 8:(i + 1) * 8],
                                        f_xnt[:, sub01 * L + kd * 512 + t128 * 128:
                                              sub01 * L + kd * 512 + (t128 + 1) * 128],
                                        c_wr[:, kd * 8:(kd + 1) * 8],
                                        start=False, stop=(kd == 7))
                            cT.inc(ins)
                        t_lg = cT.tok()

                        @blk.vector
                        def _(v, t_lg=t_lg):
                            W(v, "v", t_lg)
                            ins = v.tensor_copy(f_lg[:, :], q[7][:, 0:64])
                            cV.inc(ins)
                        t_lgev = cV.tok()
                        fst["q_free"][7] = t_lgev

                        sl = lambda i, j: f_sm[:, i * 8 + j:i * 8 + j + 1]
                        lgs = lambda i: f_lg[:, i * 8:(i + 1) * 8]
                        s8a = lambda i: f_s8a[:, i * 8:(i + 1) * 8]
                        s8b = lambda i: f_s8b[:, i * 8:(i + 1) * 8]

                        @blk.vector
                        def _(v):
                            W(v, "v", t_lgev)
                            for i in range(8):
                                v.reduce_max(sl(i, 0), lgs(i), axis=AXX)
                            for i in range(8):
                                v.tensor_scalar(sl(i, 1), sl(i, 0), -1.0, None, ALU.mult)
                            for i in range(8):
                                v.tensor_scalar(s8a(i), lgs(i), sl(i, 0), None, ALU.is_ge)
                            for i in range(8):
                                v.scalar_tensor_tensor(s8b(i), s8a(i), -1e30, lgs(i),
                                                       ALU.mult, ALU.add)
                            for i in range(8):
                                ins = v.reduce_max(sl(i, 2), s8b(i), axis=AXX)
                            cV.inc(ins)
                        t_m2x = cV.tok()

                        @blk.scalar
                        def _(s):
                            W(s, "s", t_m2x)
                            for i in range(8):
                                ins = s.activation(s8a(i), lgs(i), AF.Exp, bias=sl(i, 1))
                            cS.inc(ins)
                        t_expv = cS.tok()

                        @blk.vector
                        def _(v, p=p):
                            W(v, "v", t_expv)
                            for i in range(8):
                                v.tensor_scalar(s8b(i), lgs(i), sl(i, 2), None, ALU.is_ge)
                            for i in range(8):
                                v.tensor_tensor(s8a(i), s8a(i), s8b(i), ALU.mult)
                            for i in range(8):
                                v.reduce_sum(sl(i, 3), s8a(i), axis=AXX)
                            for i in range(8):
                                v.reciprocal(sl(i, 4), sl(i, 3))
                            for i in range(8):
                                ins = v.tensor_scalar(f_w[:, (p * 8 + i) * 8:(p * 8 + i + 1) * 8],
                                                      s8a(i), sl(i, 4), None, ALU.mult)
                            cV.inc(ins)
                        t_wall = cV.tok()
                        for i in range(8):
                            fst[("wrt", i)] = t_wall

                        @blk.tensor
                        def _(t, p=p):
                            W(t, "t", fst["q_free"][6], fst["q_free"][5])
                            for i in range(8):
                                W(t, "t", fst[("wrt", i)])
                                qq = q[6] if i < 4 else q[5]
                                ins = t.transpose(qq[0:8, (i % 4) * 128:(i % 4 + 1) * 128],
                                                  f_w[:, (p * 8 + i) * 8:(p * 8 + i + 1) * 8],
                                                  c_idf[:, :])
                            cT.inc(ins)
                        t_wtp = cT.tok()

                        @blk.vector
                        def _(v, p=p, t_wtp=t_wtp):
                            W(v, "v", t_wtp)
                            v.tensor_copy(f_wt[0:8, p * 1024:p * 1024 + 512], q[6][0:8, :])
                            ins = v.tensor_copy(f_wt[0:8, p * 1024 + 512:(p + 1) * 1024], q[5][0:8, :])
                            cV.inc(ins)
                        t_wtev = cV.tok()
                        fst["q_free"][6] = t_wtev
                        fst["q_free"][5] = t_wtev

                        for i in range(8):
                            @blk.tensor
                            def _(t, p=p, i=i):
                                W(t, "t", t_wtev, fst["q_free"][6], fst["q_free"][7])
                                t.matmul(q[6][:, :], f_wt[0:8, (p * 8 + i) * 128:(p * 8 + i + 1) * 128],
                                         c_b2[:, 0:512], start=True, stop=True)
                                ins = t.matmul(q[7][:, :], f_wt[0:8, (p * 8 + i) * 128:(p * 8 + i + 1) * 128],
                                               c_b2[:, 512:1024], start=True, stop=True)
                                cT.inc(ins)
                            t_b2 = cT.tok()

                            @blk.vector
                            def _(v, i=i, aslot=aslot, t_b2=t_b2):
                                W(v, "v", t_b2, fst["base_ld"])
                                a0 = aslot * 8 * D + i * D
                                v.tensor_tensor(f_acc[:, a0:a0 + 512], f_acc[:, a0:a0 + 512],
                                                q[6][:, :], ALU.add)
                                ins = v.tensor_tensor(f_acc[:, a0 + 512:a0 + 1024],
                                                      f_acc[:, a0 + 512:a0 + 1024], q[7][:, :], ALU.add)
                                cV.inc(ins)
                            tk = cV.tok()
                            fst["q_free"][6] = tk
                            fst["q_free"][7] = tk

                        for e in range(E):
                            emit_expert(p, e, aslot)

                        @blk.sync
                        def _(y, p=p, aslot=aslot):
                            W(y, "y", fst["evac_last"])
                            ins = y.dma_start(
                                bass.AP(out_h, p * 1024 * D, [[D, 128], [128 * D, 8], [1, D]]),
                                f_acc[:, aslot * 8 * D:(aslot + 1) * 8 * D])
                            fst["acc_free"][aslot] = dA[aslot].inc(ins)
                        fst["xnt_free"] = cT.tok()

                    def emit_expert(p, e, aslot):
                        cT0_a = cT.v
                        cS0_a = cS.v

                        @blk.gpsimd
                        def _(g, e=e):
                            for hs in range(32):
                                slot = hs % 4
                                if hs >= 4:
                                    WF(g, "g", cT, cT0_a + (hs - 4) * 2 + 2)
                                elif fst["w1_slot"][slot] is not None:
                                    W(g, "g", fst["w1_slot"][slot])
                                ins = g.dma_start(
                                    bass.AP(f_w1, slot * 1024, [[4 * 1024, 128], [128, 8], [1, 128]]),
                                    bass.AP(w1_h, (e * D) * HF + hs * 128,
                                            [[HF, 128], [128 * HF, 8], [1, 128]]))
                                fst[("w1d", hs)] = rW1[slot].inc(ins)
                                fst["w1_slot"][slot] = (cT, cT0_a + hs * 2 + 2)

                        @blk.tensor
                        def _(t):
                            for bkk in range(4):
                                W(t, "t", fst["q_free"][bkk])
                            for hs in range(32):
                                slot = hs % 4
                                W(t, "t", fst[("w1d", hs)])
                                for sub01 in range(2):
                                    bk = sub01 * 2 + hs % 2
                                    if hs >= 2:
                                        WF(t, "t", cS, cS0_a + (hs - 2) * 2 + sub01 + 1)
                                    for kd in range(8):
                                        ins = t.matmul(
                                            q[bk][:, :],
                                            f_w1[:, slot * 1024 + kd * 128:slot * 1024 + (kd + 1) * 128],
                                            f_xnt[:, sub01 * L + kd * 512:sub01 * L + (kd + 1) * 512],
                                            start=(kd == 0), stop=(kd == 7))
                                    cT.inc(ins)

                        @blk.scalar
                        def _(s, e=e):
                            for hs in range(32):
                                for sub01 in range(2):
                                    bk = sub01 * 2 + hs % 2
                                    WF(s, "s", cT, cT0_a + hs * 2 + sub01 + 1)
                                    ins = s.activation(
                                        f_ht[:, (sub01 * 32 + hs) * 512:(sub01 * 32 + hs + 1) * 512],
                                        q[bk][:, :], AF.Gelu,
                                        bias=c_b1[:, e * 32 + hs:e * 32 + hs + 1])
                                    cS.inc(ins)
                        t_ht_all = (cS, cS0_a + 64)
                        for bkk in range(8):
                            fst["q_free"][bkk] = t_ht_all

                        for dh in range(2):
                            for hs in range(32):
                                slot = (dh * 32 + hs) % 8

                                @blk.gpsimd
                                def _(g, e=e, dh=dh, hs=hs, slot=slot):
                                    W(g, "g", fst["w2_slot"][slot])
                                    ins = g.dma_start(
                                        bass.AP(f_w2, slot * 512, [[8 * 512, 128], [1, 512]]),
                                        bass.AP(w2_h, (e * HF + hs * 128) * D + dh * 512,
                                                [[D, 128], [1, 512]]))
                                    fst[("w2d", dh, hs)] = rW2[slot].inc(ins)

                                @blk.tensor
                                def _(t, dh=dh, hs=hs, slot=slot):
                                    W(t, "t", fst[("w2d", dh, hs)], t_ht_all)
                                    if hs == 0:
                                        for bkk in range(8):
                                            W(t, "t", fst["q_free"][bkk])
                                    for sub01 in range(2):
                                        for t128 in range(4):
                                            bk = sub01 * 4 + t128
                                            ins = t.matmul(
                                                q[bk][:, :],
                                                f_ht[:, (sub01 * 32 + hs) * 512 + t128 * 128:
                                                      (sub01 * 32 + hs) * 512 + (t128 + 1) * 128],
                                                f_w2[:, slot * 512:(slot + 1) * 512],
                                                start=(hs == 0), stop=(hs == 31))
                                    cT.inc(ins)
                                fst["w2_slot"][slot] = cT.tok()
                            t_bmm = cT.tok()

                            @blk.vector
                            def _(v, p=p, e=e, dh=dh, aslot=aslot, t_bmm=t_bmm):
                                W(v, "v", t_bmm)
                                for sub01 in range(2):
                                    for t128 in range(4):
                                        bk = sub01 * 4 + t128
                                        t128g = p * 8 + sub01 * 4 + t128
                                        a0 = aslot * 8 * D + (sub01 * 4 + t128) * D + dh * 512
                                        ins = v.scalar_tensor_tensor(
                                            f_acc[:, a0:a0 + 512], q[bk][:, :],
                                            f_w[:, t128g * 8 + e:t128g * 8 + e + 1],
                                            f_acc[:, a0:a0 + 512], ALU.mult, ALU.add)
                                        fst["q_free"][bk] = cV.inc(ins)
                            fst["evac_last"] = cV.tok()

                    for p in range(n_pairs):
                        emit_pair(p)

                    if "ffn" in debug:
                        @blk.sync
                        def _(y):
                            W(y, "y", cV.tok())
                            dX.inc(y.dma_start(dbg["lg"][:, 0:64], f_lg[:, :]))
                            dX.inc(y.dma_start(dbg["w"][:, :], f_w[:, :]))

                    @blk.sync
                    def _(y):
                        for chn in [dB[0], dB[1], dA[0], dA[1], dX]:
                            if chn.v > 0:
                                y.wait_ge(chn.sem, chn.v)

        else:
            with nc.Block() as blk:
                @blk.sync
                def _(y):
                    for chn in [dB[0], dB[1], dX]:
                        if chn.v > 0:
                            y.wait_ge(chn.sem, chn.v)

    return nc


_CACHE = {}


def _get_prog(key):
    if key not in _CACHE:
        n_sc, do_ffn, debug = key
        _CACHE[key] = build_program(n_sc=n_sc, do_ffn=do_ffn, debug=debug)
    return _CACHE[key]


def make_inputs(inputs):
    f32 = np.float32
    x = np.asarray(inputs["L0_attn_x"], f32)
    wipT = np.ascontiguousarray(np.asarray(inputs["L0_attn_in_proj_w"], f32).T)
    cw = np.asarray(inputs["L0_attn_conv_w"], f32)
    cdiag = np.zeros((NXT, 4, 128, 128), f32)
    ii = np.arange(128)
    cdiag[:, :, ii, ii] = cw.reshape(NXT, 128, 4).transpose(0, 2, 1)
    cdiag = cdiag.reshape(NXT * 4 * 128, 128)
    cbT = np.ascontiguousarray(np.asarray(inputs["L0_attn_conv_b"], f32).reshape(NXT, 128).T)
    dtb = np.ascontiguousarray(np.asarray(inputs["L0_attn_dt_bias"], f32).reshape(NH, 1))
    alog = np.ascontiguousarray(np.asarray(inputs["L0_attn_A_log"], f32).reshape(NH, 1))
    dbc = np.ascontiguousarray(np.broadcast_to(
        np.asarray(inputs["L0_attn_D"], f32).reshape(1, NH), (128, NH)))
    nwT = np.ascontiguousarray(np.asarray(inputs["L0_attn_norm_w"], f32).reshape(16, 128).T)
    wopT = np.ascontiguousarray(np.asarray(inputs["L0_attn_out_proj_w"], f32).T)
    w1 = np.ascontiguousarray(np.asarray(inputs["L0_ffn_W1"], f32).reshape(E * D, HF))
    b1T = np.ascontiguousarray(np.asarray(inputs["L0_ffn_b1"], f32)
                               .reshape(E, 32, 128).transpose(0, 2, 1)).reshape(E * 128, 32)
    w2 = np.ascontiguousarray(np.asarray(inputs["L0_ffn_W2"], f32).reshape(E * HF, D))
    b2 = np.ascontiguousarray(np.asarray(inputs["L0_ffn_b2"], f32))
    wr = np.ascontiguousarray(np.asarray(inputs["L0_ffn_Wr"], f32))
    br = np.ascontiguousarray(np.asarray(inputs["L0_ffn_br"], f32).reshape(1, E))

    s = np.arange(128)
    u01 = (s[:, None] <= s[None, :]).astype(f32)
    a1s = (s[:, None] > s[None, :]).astype(f32)
    idf = np.eye(128, dtype=f32)
    ones1 = np.ones((1, 128), f32)

    shared = dict(wipT=wipT, cdiag=cdiag, cbT=cbT, dtb=dtb, alog=alog, dbc=dbc,
                  nwT=nwT, wopT=wopT, w1=w1, b1T=b1T, w2=w2, b2=b2, wr=wr, br=br,
                  u01=u01, a1s=a1s, idf=idf, ones1=ones1)
    in_maps = []
    for b in range(8):
        d = dict(shared)
        d["x"] = np.ascontiguousarray(x[b])
        in_maps.append(d)
    return in_maps


def kernel(**inputs):
    nc = _get_prog((8, True, ()))
    in_maps = make_inputs(inputs)
    res = run_bass_kernel_spmd(nc, in_maps, core_ids=list(range(8)))
    return np.stack([np.asarray(r["out"], np.float32) for r in res.results], axis=0)
